# version 08
# baseline (speedup 1.0000x reference)
import numpy as np
from contextlib import ExitStack

import concourse.bass as bass
import concourse.mybir as mybir
from concourse.bass_utils import run_bass_kernel_spmd
from concourse.bass import IndirectOffsetOnAxis

F32 = mybir.dt.float32
BF16 = mybir.dt.bfloat16
I32 = mybir.dt.int32
AF = mybir.ActivationFunctionType
ALU = mybir.AluOpType
AX = mybir.AxisListType.X

B, T, C = 2, 4096, 2048
NH, HS, NL = 16, 128, 512
NT = B * T
NCORE = 8
OWN = 1024
WIN = 4
KC = 32
NE, NI, TOPK = 32, 1024, 4
CAPL = 192
LN_EPS = 1e-5
RMS_EPS = 1e-6
ISQ = float(1.0 / np.sqrt(HS))
EXPB = -4.0


def own_index(c: int) -> np.ndarray:
    idx = []
    for b in range(B):
        for i in range(WIN):
            for u in range(7, -1, -1):
                base = b * T + 1024 * i + 128 * u + 16 * c
                idx.extend(range(base, base + 16))
    return np.asarray(idx, dtype=np.int64)


class Ctr:
    def __init__(self, sem):
        self.sem = sem
        self.val = 0

    def inc(self, instr, n=1):
        instr.then_inc(self.sem, n)
        self.val += n
        return self.val

    def dma(self, instr):
        return self.inc(instr, 16)


class Build:
    def __init__(self, taps=(), upto="all"):
        self.taps = set(taps)
        self.upto = upto
        nc = bass.Bass("TRN2", target_bir_lowering=False)
        self.nc = nc
        self.stack = ExitStack()
        d = self.dram_in = {}
        for name, shape in [
            ("xrows", [OWN, C]),
            ("xtown", [128, 16 * OWN]),
            ("xt_tiled", [NT // 128, 128, 16, 128]),
            ("wq_s", [16, 128, 16, 128]),
            ("wqt_s", [16, 128, 16, 128]),
            ("wdkvt_s", [128, 16, NL]),
            ("wuk_s", [128, 16, NL]),
            ("wuv_s", [128, 16, NL]),
            ("wot", [C, C]),
            ("gatet_s", [128, 16, NE]),
            ("normw", [1, C]),
            ("sgw_s", [8, 128, 16, 128]),
            ("suw_s", [8, 128, 16, 128]),
            ("sdw", [NI, C]),
            ("w13", [NE, C, 2 * NI]),
            ("w2", [NE, NI, C]),
            ("m16", [128, 16]),
        ]:
            d[name] = nc.dram_tensor(name, shape, F32, kind="ExternalInput").ap()

        self.orows = nc.dram_tensor("orows", [OWN, C], F32, kind="ExternalOutput").ap()

        self.CKV = nc.dram_tensor("CKV", [NT, NL], BF16)
        self.RSPILL = nc.dram_tensor("RSPILL", [OWN, C], F32)
        self.GBUF = nc.dram_tensor("GBUF", [NE * CAPL, C], BF16)
        self.IDINFO = nc.dram_tensor("IDINFO", [NE * CAPL, 8], I32)
        self.CONTRIB = nc.dram_tensor("CONTRIB", [TOPK * OWN, C], BF16)
        self.SHARED = nc.dram_tensor("SHARED", [OWN, C], BF16)
        self.TAPD = nc.dram_tensor("TAPD", [128, 16 * OWN], BF16)
        self.QTD = nc.dram_tensor("QTD", [128, 16 * OWN], BF16)

        self.tap_out = {}
        for t in self.taps:
            shp, dt = {
                "ckv": ([NT, NL], F32), "keff": ([128, 16 * NL], F32),
                "veff": ([4, 128, C], F32), "qT": ([128, 16 * OWN], F32),
                "attn_r": ([OWN, C], F32), "xnorm": ([OWN, C], F32),
                "logits": ([OWN, NE], F32), "routew": ([OWN, 4], F32),
                "routeid": ([OWN, 4], F32), "pos4": ([OWN, 4], F32),
                "shared": ([OWN, C], F32), "gbuf": ([NE * CAPL, C], F32),
                "idinfo": ([NE * CAPL, 8], I32), "contrib": ([TOPK * OWN, C], F32),
                "dbg": ([128, 3072], BF16),
                "rsp": ([OWN, C], F32),
            }[t]
            self.tap_out[t] = nc.dram_tensor("tap_" + t, shp, dt,
                                             kind="ExternalOutput").ap()

        self.sems = {}
        for s in ["dXT", "dW", "dP", "dT", "tM", "vE", "sE", "gI", "stO",
                  "dW2", "bar", "pA", "pB", "qA", "qB", "rA", "rB",
                  "tA", "tB", "gF", "gS", "wA2", "wB2", "wC2", "wD2", "cW"]:
            self.sems[s] = Ctr(self.stack.enter_context(nc.semaphore(s)))
        self.nbar = 0

    def sb(self, name, shape, dtype, stack=None):
        return (stack or self.stack).enter_context(
            self.nc.sbuf_tensor(name, shape, dtype))

    def psum(self, name, shape, dtype=F32, stack=None):
        return (stack or self.stack).enter_context(
            self.nc.psum_tensor(name, shape, dtype))

    def sem(self, name):
        return self.sems[name]

    def barrier(self):
        block = self.block
        bar = self.sem("bar")
        self.nbar += 1
        tgt = 5 * self.nbar

        for engname in ["tensor", "vector", "scalar", "gpsimd", "sync"]:
            @block.__getattribute__(engname)
            def _(e):
                e.sem_inc(bar.sem, 1)
                e.wait_ge(bar.sem, tgt)
        bar.val = tgt

    def build(self):
        nc = self.nc
        with self.stack:
            with nc.Block() as block:
                self.block = block
                r_live = ExitStack()
                with nc.named_scope("ph0"):
                    self.ph0(r_live)
                with ExitStack() as att_live:
                    with ExitStack() as prep_stack:
                        ps8 = [self.psum(f"pp{i}", [128, NL], stack=prep_stack)
                               for i in range(8)]
                        with nc.named_scope("ph1_ckv"):
                            self.ph1_ckv(ps8, prep_stack)
                        with nc.named_scope("ph2_prep"):
                            self.ph2_prep(ps8, att_live, prep_stack)
                        if self.upto == "prep":
                            self.emit_taps_prep()
                            self.finish()
                            return nc
                        self.barrier()
                    with nc.named_scope("ph4_attn"):
                        self.ph4_attention(att_live)
                    if self.upto == "attnraw":
                        self.emit_taps_attn(self.attn_done_vE)
                        self.finish()
                        return nc
                    with nc.named_scope("ph5_ln1"):
                        self.ph5_ln1()
                    if self.upto == "attn":
                        self.emit_taps_attn(self.ln1_done_vE)
                        self.finish()
                        return nc
                    self.barrier()
                with ExitStack() as ffn_stack:
                    with nc.named_scope("ph6_ffnprep"):
                        self.ph6_ffnprep(ffn_stack)
                    if self.upto == "ffn6":
                        self.emit_taps_ffnprep()
                        self.finish()
                        return nc
                    with nc.named_scope("ph7_shared"):
                        self.ph7_shared(ffn_stack)
                    with nc.named_scope("ph8_spill"):
                        self.ph8_spill_r(ffn_stack)
                    if self.upto == "ffnprep":
                        self.emit_taps_ffnprep()
                        self.finish()
                        return nc
                    self.barrier()
                r_live.close()
                with ExitStack() as moe_stack:
                    with nc.named_scope("ph9_moe"):
                        self.ph9_moe(moe_stack)
                    self.barrier()
                with ExitStack() as ret_stack:
                    with nc.named_scope("ph10_ret"):
                        self.ph10_return(ret_stack)
                if self.taps:
                    self.emit_taps_full()
                self.finish()
        return nc

    def ph0(self, r_live):
        nc, block = self.nc, self.block
        d = self.dram_in
        self.gatet_sb = self.sb("gatet_sb", [128, 16 * NE], BF16)
        self.m16_sb = self.sb("m16_sb", [128, 16], BF16)
        self.consts = self.sb("consts", [128, 8], F32)
        self.lncols = self.sb("lncols", [128, 8], F32)
        self.ones_sb = self.sb("ones_sb", [128, 128], BF16)
        self.iota32 = self.sb("iota32", [128, NE], F32)
        self.iotap = self.sb("iotap", [128, 1], F32)
        self.ident = self.sb("ident", [128, 128], BF16)
        self.identF = self.sb("identF", [128, 128], F32)
        self.cumL = self.sb("cumL", [128, 128], BF16)
        self.wids = self.sb("wids", [128, 8 * 8], F32)
        self.R = self.sb("R", [128, 8 * C], F32, r_live)

        dP, dW, vE = self.sem("dP"), self.sem("dW"), self.sem("vE")

        @block.sync
        def _(sy):
            dP.dma(sy.dma_start(
                out=self.R[:].rearrange("p (tt c) -> p tt c", tt=8),
                in_=d["xrows"].rearrange("(tt p) c -> p tt c", p=128)))
        self.dP_r_loaded = dP.val

        @block.gpsimd
        def _(g):
            dW.dma(g.dma_start(out=self.gatet_sb[:],
                               in_=d["gatet_s"].rearrange("p cc e -> p (cc e)")))
            dW.dma(g.dma_start(out=self.m16_sb[:], in_=d["m16"]))
            g.memset(self.ones_sb[:], 1.0)
            g.iota(self.iota32[:], [[1, NE]], channel_multiplier=0,
                   allow_small_or_imprecise_dtypes=True)
            g.iota(self.iotap[:], [[0, 1]], channel_multiplier=1,
                   allow_small_or_imprecise_dtypes=True)
            gM = self.sem("gF")
            gM.inc(g.memset(self.ident[:], 0.0))
            gM.inc(g.memset(self.identF[:], 0.0))
            gM.inc(g.memset(self.cumL[:], 0.0))
            g.wait_ge(gM.sem, gM.val)
            g.affine_select(self.ident[:], self.ident[:], pattern=[[-1, 128]],
                            compare_op=ALU.not_equal, fill=1.0, base=0,
                            channel_multiplier=1)
            g.affine_select(self.identF[:], self.identF[:], pattern=[[-1, 128]],
                            compare_op=ALU.not_equal, fill=1.0, base=0,
                            channel_multiplier=1)
            ins = g.affine_select(self.cumL[:], self.cumL[:], pattern=[[-1, 128]],
                                  compare_op=ALU.is_ge, fill=1.0, base=0,
                                  channel_multiplier=1)
            vE.inc(ins)
        self.vE_consts = vE.val

        @block.vector
        def _(v):
            v.memset(self.consts[:, 0:1], EXPB)
            vE.inc(v.memset(self.consts[:, 1:2], 0.0))

    def ph1_ckv(self, ps8, stk):
        nc, block = self.nc, self.block
        d = self.dram_in
        with ExitStack() as sub:
            wdkvt_sb = self.sb("wdkvt_sb", [128, 16 * NL], BF16, sub)
            xt_sb = [self.sb(f"xt{i}", [128, C], BF16, sub) for i in range(2)]
            stage = [self.sb(f"ckvst{i}", [128, NL], BF16, sub) for i in range(2)]
            ps_ckv = ps8[0:2]
            tM, vE, dP, dW2 = (self.sem(s) for s in
                               ["tM", "vE", "dP", "dW2"])
            pp = [self.sem("pA"), self.sem("pB")]
            rr = [self.sem("rA"), self.sem("rB")]
            NTC = NT // 128
            tm0, ve0 = tM.val, vE.val
            pp0 = [p.val for p in pp]
            rr0 = [r.val for r in rr]
            dwk = dW2.val

            @block.gpsimd
            def _(g):
                dW2.dma(g.dma_start(
                    out=wdkvt_sb[:],
                    in_=d["wdkvt_s"].rearrange("p cc l -> p (cc l)")))
                for tc in range(NTC):
                    g.wait_ge(tM.sem, max(0, tm0 + tc - 1))
                    pp[tc % 2].dma(g.dma_start(
                        out=xt_sb[tc % 2][:],
                        in_=d["xt_tiled"][tc].rearrange("p cc t -> p (cc t)")))

            @block.tensor
            def _(t):
                t.wait_ge(dW2.sem, dwk + 16)
                for tc in range(NTC):
                    t.wait_ge(pp[tc % 2].sem, pp0[tc % 2] + 16 * (tc // 2 + 1))
                    if tc >= 2:
                        t.wait_ge(vE.sem, ve0 + tc - 1)
                    for cc in range(16):
                        ins = t.matmul(ps_ckv[tc % 2][:],
                                       xt_sb[tc % 2][:, cc * 128:(cc + 1) * 128],
                                       wdkvt_sb[:, cc * NL:(cc + 1) * NL],
                                       start=(cc == 0), stop=(cc == 15))
                    tM.inc(ins)

            @block.vector
            def _(v):
                for tc in range(NTC):
                    v.wait_ge(tM.sem, tm0 + tc + 1)
                    if tc >= 2:
                        v.wait_ge(rr[tc % 2].sem, rr0[tc % 2] + 16 * (tc // 2))
                    vE.inc(v.tensor_copy(stage[tc % 2][:], ps_ckv[tc % 2][:]))

            @block.sync
            def _(sy):
                for tc in range(NTC):
                    sy.wait_ge(vE.sem, ve0 + tc + 1)
                    rr[tc % 2].dma(sy.dma_start(
                        out=self.CKV[tc * 128:(tc + 1) * 128, :],
                        in_=stage[tc % 2][:]))
            self.ckv_done_dP = dP.val
            self.ckv_done_st = (rr[0].val, rr[1].val)
            self.ckv_done_tM = tM.val
            self.ckv_done_vE = vE.val

    def ph2_prep(self, ps8, live, stk):
        nc, block = self.nc, self.block
        d = self.dram_in
        self.keff_sb = self.sb("keff_sb", [128, 16 * NL], BF16, live)
        self.veff_sb = [self.sb(f"veff{lc}", [128, C], BF16, live)
                        for lc in range(4)]
        self.qT_sb = self.sb("qT_sb", [128, 16 * OWN], BF16, stk)
        dW2, tM, vE = (self.sem(s) for s in ["dW2", "tM", "vE"])

        with ExitStack() as sub:
            wuk_sb = self.sb("wuk_sb", [128, 16 * NL], BF16, sub)
            wbuf = [self.sb(f"wbufa{i}", [128, C], BF16, sub) for i in range(2)]
            pp = [self.sem("pA"), self.sem("pB")]
            dw0, tm0, ve0 = dW2.val, tM.val, vE.val
            pp0 = [p.val for p in pp]

            @block.gpsimd
            def _(g):
                g.wait_ge(tM.sem, self.ckv_done_tM)
                g.wait_ge(self.sem("rA").sem, self.ckv_done_st[0])
                g.wait_ge(self.sem("rB").sem, self.ckv_done_st[1])
                dW2.dma(g.dma_start(
                    out=wuk_sb[:],
                    in_=d["wuk_s"].rearrange("p cc l -> p (cc l)")))
                for jc in range(16):
                    g.wait_ge(tM.sem, max(0, tm0 + jc - 1))
                    pp[jc % 2].dma(g.dma_start(
                        out=wbuf[jc % 2][:],
                        in_=d["wq_s"][jc].rearrange("p ic pj -> p (ic pj)")))

            @block.tensor
            def _(t):
                t.wait_ge(vE.sem, self.ckv_done_vE)
                t.wait_ge(dW2.sem, dw0 + 16)
                for jc in range(16):
                    t.wait_ge(pp[jc % 2].sem, pp0[jc % 2] + 16 * (jc // 2 + 1))
                    if jc >= 2:
                        t.wait_ge(vE.sem, ve0 + jc - 1)
                    for ic in range(16):
                        ins = t.matmul(ps8[jc % 2][:],
                                       wbuf[jc % 2][:, ic * 128:(ic + 1) * 128],
                                       wuk_sb[:, ic * NL:(ic + 1) * NL],
                                       start=(ic == 0), stop=(ic == 15))
                    tM.inc(ins)

            @block.vector
            def _(v):
                for jc in range(16):
                    v.wait_ge(tM.sem, tm0 + jc + 1)
                    vE.inc(v.tensor_copy(self.keff_sb[:, jc * NL:(jc + 1) * NL],
                                         ps8[jc % 2][:]))
            keff_tM = tM.val

        with ExitStack() as sub:
            wuv_sb = self.sb("wuv_sb", [128, 16 * NL], BF16, sub)
            wbuf = [self.sb(f"wbufb{i}", [128, C], BF16, sub) for i in range(2)]
            pp = [self.sem("pA"), self.sem("pB")]
            dwv0 = dW2.val

            @block.gpsimd
            def _(g):
                g.wait_ge(tM.sem, keff_tM)
                dW2.dma(g.dma_start(
                    out=wuv_sb[:],
                    in_=d["wuv_s"].rearrange("p cc l -> p (cc l)")))

            for vpass in range(2):
                dwv, tmv, vev = dW2.val, tM.val, vE.val

                ppv = [p.val for p in pp]

                @block.gpsimd
                def _(g, tmv=tmv):
                    for mc in range(16):
                        g.wait_ge(tM.sem, max(0, tmv + mc - 1))
                        pp[mc % 2].dma(g.dma_start(
                            out=wbuf[mc % 2][:],
                            in_=d["wot"][mc * 128:(mc + 1) * 128, :]))

                @block.tensor
                def _(t, vpass=vpass, ppv=ppv, vev=vev):
                    t.wait_ge(dW2.sem, dwv0 + 16)
                    t.wait_ge(vE.sem, vev)
                    for mc in range(16):
                        t.wait_ge(pp[mc % 2].sem, ppv[mc % 2] + 16 * (mc // 2 + 1))
                        ins = None
                        for lc in range(4):
                            for gsub in range(2):
                                grp = vpass * 2 + gsub
                                ins = t.matmul(
                                    ps8[lc * 2 + gsub][:],
                                    wuv_sb[:, mc * NL + lc * 128:
                                           mc * NL + (lc + 1) * 128],
                                    wbuf[mc % 2][:, grp * 512:(grp + 1) * 512],
                                    start=(mc == 0), stop=(mc == 15))
                        tM.inc(ins)

                @block.vector
                def _(v, vpass=vpass, tmv=tmv):
                    v.wait_ge(tM.sem, tmv + 16)
                    ins = None
                    for lc in range(4):
                        for gsub in range(2):
                            grp = vpass * 2 + gsub
                            ins = v.tensor_copy(
                                self.veff_sb[lc][:, grp * 512:(grp + 1) * 512],
                                ps8[lc * 2 + gsub][:])
                    vE.inc(ins)
            veff_tM = tM.val

        with ExitStack() as sub:
            self.xtown_sb = self.sb("xtown_sb", [128, 16 * OWN], BF16, stk)
            wbuf = [self.sb(f"wbufc{i}", [128, C], BF16, sub) for i in range(2)]
            pp = [self.sem("pA"), self.sem("pB")]
            dwq, tmq, veq = dW2.val, tM.val, vE.val
            ppq = [p.val for p in pp]

            @block.gpsimd
            def _(g):
                g.wait_ge(tM.sem, veff_tM)
                dW2.dma(g.dma_start(out=self.xtown_sb[:], in_=d["xtown"]))
                for ic in range(16):
                    g.wait_ge(tM.sem, max(0, tmq + ic - 1))
                    pp[ic % 2].dma(g.dma_start(
                        out=wbuf[ic % 2][:],
                        in_=d["wqt_s"][ic].rearrange("p jc pi -> p (jc pi)")))

            @block.tensor
            def _(t):
                t.wait_ge(vE.sem, veq)
                t.wait_ge(dW2.sem, dwq + 16)
                for ic in range(16):
                    t.wait_ge(pp[ic % 2].sem, ppq[ic % 2] + 16 * (ic // 2 + 1))
                    if ic >= 2:
                        t.wait_ge(vE.sem, veq + 2 * (ic - 1))
                    for jc in range(16):
                        st = wbuf[ic % 2][:, jc * 128:(jc + 1) * 128]
                        t.matmul(ps8[(ic % 2) * 2][:], st,
                                 self.xtown_sb[:, jc * OWN: jc * OWN + 512],
                                 start=(jc == 0), stop=(jc == 15))
                        ins = t.matmul(
                            ps8[(ic % 2) * 2 + 1][:], st,
                            self.xtown_sb[:, jc * OWN + 512:(jc + 1) * OWN],
                            start=(jc == 0), stop=(jc == 15))
                    tM.inc(ins)

            @block.vector
            def _(v):
                for ic in range(16):
                    v.wait_ge(tM.sem, tmq + ic + 1)
                    v.tensor_copy(self.qT_sb[:, ic * OWN: ic * OWN + 512],
                                  ps8[(ic % 2) * 2][:])
                    vE.inc(v.tensor_copy(
                        self.qT_sb[:, ic * OWN + 512:(ic + 1) * OWN],
                        ps8[(ic % 2) * 2 + 1][:]), 2)
            qt_vE = vE.val

            @block.sync
            def _(sy):
                sy.wait_ge(vE.sem, qt_vE)
                self.sem("dP").dma(sy.dma_start(out=self.QTD[:, :],
                                                in_=self.qT_sb[:]))
        self.qtd_done_dP = self.sem("dP").val
        self.prep_done_tM = tM.val
        self.prep_done_vE = vE.val


    def ph4_attention(self, stk):
        nc, block = self.nc, self.block
        for nm in ["aSC", "aYL", "aEX", "aMK"]:
            if nm not in self.sems:
                self.sems[nm] = Ctr(self.stack.enter_context(nc.semaphore(nm)))
        aSC, aYL = self.sem("aSC"), self.sem("aYL")
        dP, dT = self.sem("dP"), self.sem("dT")

        self.ckv_sb = self.sb("ckv_sb", [128, KC * NL], BF16, stk)
        self.ckvT_sb = self.sb("ckvT_sb", [128, 4 * T], BF16, stk)
        self.qTh = self.sb("qTh", [128, 16 * 512], BF16, stk)
        self.QLT = self.sb("QLT", [128, 4 * 512], BF16, stk)
        self.PT = [self.sb(f"PT{i}", [128, 512], BF16, stk) for i in range(2)]
        self.recip_row = self.sb("recip_row", [1, 512], F32, stk)
        self.recipB = self.sb("recipB", [128, 512], F32, stk)
        self.ylnorm = self.sb("ylnorm", [128, 4 * 512], BF16, stk)
        self.yTe = self.sb("yTe", [128, 512], BF16, stk)
        self.ones_f32 = self.sb("ones_f32", [1, 128], F32, stk)
        self.sq_sb = self.sb("sq_sb", [128, C // 2], F32, stk)

        self.dbg_sb = self.sb("dbg_sb", [128, 3072], BF16, stk) \
            if "dbg" in self.tap_out else None
        self.ps_s = [self.psum(f"ps_s{i}", [128, 512], stack=stk) for i in range(2)]
        self.ps_y = [self.psum(f"ps_y{i}", [128, 512], stack=stk) for i in range(4)]
        self.ps_m = self.psum("ps_m", [128, 512], stack=stk)
        self.psb = self.psum("psb", [128, 1024], BF16, stack=stk)

        @block.vector
        def _(v):
            v.wait_ge(self.sem("dW").sem, 32)
            v.memset(self.ones_f32[:], 1.0)

        for b in range(B):
            dp_b = dP.val
            dt_b = dT.val

            @block.sync
            def _(sy, b=b):
                if b > 0:
                    sy.wait_ge(aYL.sem, aYL.val)
                    sy.wait_ge(aSC.sem, aSC.val)
                else:
                    sy.wait_ge(self.sem("rA").sem, self.ckv_done_st[0])
                    sy.wait_ge(self.sem("rB").sem, self.ckv_done_st[1])
                    sy.wait_ge(dP.sem, self.qtd_done_dP)
                dP.dma(sy.dma_start(
                    out=self.ckv_sb[:].rearrange("p (kc l) -> p kc l", kc=KC),
                    in_=self.CKV[b * T:(b + 1) * T, :]
                        .rearrange("(kc p) l -> p kc l", p=128)))
                dP.dma(sy.dma_start(
                    out=self.qTh[:].rearrange("p (ic t) -> p ic t", ic=16),
                    in_=self.QTD[:, :].rearrange("p (ic bt t) -> p ic bt t",
                                                 ic=16, bt=2)[:, :, b]))

            @block.sync
            def _(s, b=b):
                if b > 0:
                    s.wait_ge(aYL.sem, aYL.val)
                    s.wait_ge(aSC.sem, aSC.val)
                else:
                    s.wait_ge(self.sem("rA").sem, self.ckv_done_st[0])
                    s.wait_ge(self.sem("rB").sem, self.ckv_done_st[1])
                for lc in range(4):
                    dT.dma(s.dma_start_transpose(
                        self.ckvT_sb[:, lc * T:(lc + 1) * T],
                        self.CKV[b * T:(b + 1) * T, lc * 128:(lc + 1) * 128]))

            for h in range(NH):
                self._attn_bh(b, h, dp_b, dt_b)

        self.attn_done_vE = self.sem("vE").val

    @staticmethod
    def _regions(j):
        w, up = j // 8, j % 8
        pfx = 16 * (8 - up)
        if up == 0:
            return [(128 * w, 512)]
        out = [(128 * w, 128 * w + pfx)]
        if w < 3:
            out.append((128 * (w + 1), 512))
        return out

    def _attn_bh(self, b, h, dp_b, dt_b):
        nc, block = self.nc, self.block
        aSC, aYL, aEX, aMK = (self.sem(s) for s in ["aSC", "aYL", "aEX", "aMK"])
        vE, sE, dP, dT = (self.sem(s) for s in ["vE", "sE", "dP", "dT"])
        ps_s, ps_y, ps_m, psb = self.ps_s, self.ps_y, self.ps_m, self.psb
        PT, QLT = self.PT, self.QLT
        ex_base, mk_base, yl_base, v_start = aEX.val, aMK.val, aYL.val, vE.val

        veq0 = vE.val

        @block.tensor
        def _(t):
            t.wait_ge(dP.sem, dp_b + 32)
            t.wait_ge(dT.sem, dt_b + 64)
            t.wait_ge(aEX.sem, max(0, ex_base))
            t.wait_ge(vE.sem, v_start)
            for lc in range(4):
                if lc >= 2:
                    t.wait_ge(vE.sem, veq0 + lc - 1)
                aSC.inc(t.matmul(ps_s[lc % 2][:],
                                 self.keff_sb[:, h * NL + lc * 128:
                                              h * NL + (lc + 1) * 128],
                                 self.qTh[:, h * 512:(h + 1) * 512],
                                 start=True, stop=True))
        ql_mm = aSC.val

        @block.vector
        def _(v):
            for lc in range(4):
                v.wait_ge(aSC.sem, ql_mm - 3 + lc)
                vE.inc(v.tensor_copy(QLT[:, lc * 512:(lc + 1) * 512],
                                     ps_s[lc % 2][:]))
        ql_ev = vE.val
        assert ql_ev == veq0 + 4

        nj = KC
        sc_base = aSC.val

        def emit_scores(t, j):
            if j < 2:
                t.wait_ge(vE.sem, ql_ev)
            t.wait_ge(aEX.sem, max(0, ex_base + j - 1))
            ins = None
            for (q0, q1) in self._regions(j):
                for lc in range(4):
                    st = self.ckvT_sb[:, lc * T + j * 128:lc * T + (j + 1) * 128]
                    ins = t.matmul(ps_s[j % 2][:, q0:q1],
                                   st, QLT[:, lc * 512 + q0:lc * 512 + q1],
                                   start=(lc == 0), stop=(lc == 3))
            aSC.inc(ins)

        def emit_ylat(t, j):
            t.wait_ge(aMK.sem, mk_base + j + 1)
            if j == 0:
                t.wait_ge(vE.sem, v_start)
            ins = None
            last = (j == KC - 1)
            for (q0, q1) in self._regions(j):
                for lc in range(4):
                    st = self.ckv_sb[:, j * NL + lc * 128:j * NL + (lc + 1) * 128]
                    ins = t.matmul(ps_y[lc][:, q0:q1], st, PT[j % 2][:, q0:q1],
                                   start=(j == 0), stop=last)
                ins = t.matmul(ps_m[0:1, q0:q1], self.ones_sb[:, 0:1],
                               PT[j % 2][:, q0:q1], start=(j == 0), stop=last)
            aYL.inc(ins)

        @block.tensor
        def _(t):
            emit_scores(t, 0)
            for j in range(nj):
                if j + 1 < nj:
                    emit_scores(t, j + 1)
                emit_ylat(t, j)

        @block.scalar
        def _(s):
            for j in range(nj):
                qb = 128 * (j // 8)
                s.wait_ge(aSC.sem, sc_base + j + 1)
                if j >= 2:
                    s.wait_ge(aYL.sem, yl_base + j - 1)
                aEX.inc(s.activation(PT[j % 2][:, qb:512],
                                     ps_s[j % 2][:, qb:512],
                                     AF.Exp, bias=self.consts[:, 0:1],
                                     scale=ISQ))

        @block.vector
        def _(v):
            for j in range(nj):
                w, up = j // 8, j % 8
                pfx, qb = 16 * (8 - up), 128 * w
                v.wait_ge(aEX.sem, ex_base + j + 1)
                aMK.inc(v.tensor_mul(PT[j % 2][:, qb + pfx - 16:qb + pfx],
                                     PT[j % 2][:, qb + pfx - 16:qb + pfx],
                                     self.m16_sb[:]))

        @block.vector
        def _(v):
            v.wait_ge(aYL.sem, yl_base + nj)
            if b == 0 and h == 0 and "dbg" in self.tap_out:
                v.tensor_copy(self.dbg_sb[0:1, 0:512], ps_m[0:1, 0:512])
                v.tensor_copy(self.dbg_sb[:, 512:1024], ps_y[0][:])
                v.tensor_copy(self.dbg_sb[:, 1024:1536], QLT[:, 0:512])
                v.tensor_copy(self.dbg_sb[:, 1536:2048], PT[(nj - 1) % 2][:])
            vE.inc(v.reciprocal(self.recip_row[:], ps_m[0:1, 0:512]))
        rc_done = vE.val

        @block.tensor
        def _(t):
            t.wait_ge(vE.sem, rc_done)
            aSC.inc(t.matmul(ps_m[:, 0:512], self.ones_f32[:],
                             self.recip_row[:], start=True, stop=True))
        bc_done = aSC.val

        @block.vector
        def _(v):
            v.wait_ge(aSC.sem, bc_done)
            vE.inc(v.tensor_copy(self.recipB[:], ps_m[:, 0:512]))

        rb_done = vE.val

        @block.vector
        def _(v):
            v.wait_ge(vE.sem, rb_done)
            for lc in range(4):
                vE.inc(v.tensor_mul(self.ylnorm[:, lc * 512:(lc + 1) * 512],
                                    ps_y[lc][:], self.recipB[:]))
        div_done = vE.val

        if b == 0 and h == 0 and "dbg" in self.tap_out:
            @block.vector
            def _(v):
                v.tensor_copy(self.dbg_sb[:, 2048:2560], self.ylnorm[:, 0:512])
                vE.inc(v.tensor_copy(self.dbg_sb[:, 2560:3072], self.recipB[:]))

            @block.sync
            def _(sy):
                sy.wait_ge(vE.sem, vE.val)
                self.sem("stO").dma(sy.dma_start(out=self.tap_out["dbg"][:, :],
                                                 in_=self.dbg_sb[:]))

        @block.tensor
        def _(t):
            t.wait_ge(vE.sem, div_done)
            ins = None
            for lc in range(4):
                ins = t.matmul(ps_m[:, 0:512],
                               self.veff_sb[lc][:, h * 128:(h + 1) * 128],
                               self.ylnorm[:, lc * 512:(lc + 1) * 512],
                               start=(lc == 0), stop=(lc == 3))
            aSC.inc(ins)
        pj_done = aSC.val

        @block.vector
        def _(v):
            v.wait_ge(aSC.sem, pj_done)
            vE.inc(v.tensor_copy(self.yTe[:], ps_m[:, 0:512]))
        ye_done = vE.val

        @block.tensor
        def _(t):
            t.wait_ge(vE.sem, ye_done)
            ins = None
            for w in range(WIN):
                ins = t.transpose(psb[:, w * 128:(w + 1) * 128],
                                  self.yTe[:, w * 128:(w + 1) * 128],
                                  self.ident[:])
            aSC.inc(ins)
        tr_done = aSC.val

        @block.vector
        def _(v):
            v.wait_ge(aSC.sem, tr_done)
            for w in range(WIN):
                rt = b * WIN + w
                vE.inc(v.tensor_add(
                    self.R[:, rt * C + h * 128: rt * C + (h + 1) * 128],
                    self.R[:, rt * C + h * 128: rt * C + (h + 1) * 128],
                    psb[:, w * 128:(w + 1) * 128]))

    def ph5_ln1(self):
        self._layernorm_tiles(self.R, list(range(8)), LN_EPS, None)

    def _layernorm_tiles(self, Rt, tiles, eps, out_dma=None):
        nc, block = self.nc, self.block
        vE, sE = self.sem("vE"), self.sem("sE")
        cols = self.lncols
        base_vE = vE.val
        H2 = C // 2

        for k, tt in enumerate(tiles):
            sl = slice(tt * C, (tt + 1) * C)
            slA = slice(tt * C, tt * C + H2)
            slB = slice(tt * C + H2, (tt + 1) * C)
            tile_vE = vE.val

            @block.scalar
            def _(s, slA=slA, tile_vE=tile_vE):
                s.wait_ge(vE.sem, max(base_vE, tile_vE - 4))
                sE.inc(s.activation(self.sq_sb[:], Rt[:, slA], AF.Square))
            sqa_done = sE.val

            @block.vector
            def _(v, sl=sl, sqa_done=sqa_done):
                v.wait_ge(vE.sem, base_vE)
                vE.inc(v.reduce_sum(cols[:, 0:1], Rt[:, sl], axis=AX))
                v.wait_ge(sE.sem, sqa_done)
                vE.inc(v.reduce_sum(cols[:, 1:2], self.sq_sb[:], axis=AX))
            ra_done = vE.val

            @block.scalar
            def _(s, slB=slB, ra_done=ra_done):
                s.wait_ge(vE.sem, ra_done)
                sE.inc(s.activation(self.sq_sb[:], Rt[:, slB], AF.Square))
            sqb_done = sE.val

            @block.vector
            def _(v, sqb_done=sqb_done, ra_done=ra_done):
                v.wait_ge(sE.sem, sqb_done)
                vE.inc(v.reduce_sum(cols[:, 4:5], self.sq_sb[:], axis=AX))
                v.wait_ge(vE.sem, vE.val)
                vE.inc(v.tensor_add(cols[:, 1:2], cols[:, 1:2], cols[:, 4:5]))
                vE.inc(v.tensor_scalar(cols[:, 2:3], cols[:, 0:1], 1.0 / C,
                                       None, ALU.mult))
                v.wait_ge(vE.sem, vE.val)
                vE.inc(v.tensor_scalar(cols[:, 3:4], cols[:, 1:2], 1.0 / C,
                                       None, ALU.mult))
                vE.inc(v.tensor_mul(cols[:, 4:5], cols[:, 2:3], cols[:, 2:3]))
                v.wait_ge(vE.sem, vE.val)
                vE.inc(v.tensor_sub(cols[:, 5:6], cols[:, 3:4], cols[:, 4:5]))
                v.wait_ge(vE.sem, vE.val)
                vE.inc(v.tensor_scalar(cols[:, 5:6], cols[:, 5:6], float(eps),
                                       None, ALU.add))
            var_done = vE.val

            @block.scalar
            def _(s, var_done=var_done):
                s.wait_ge(vE.sem, var_done)
                sE.inc(s.activation(cols[:, 6:7], cols[:, 5:6], AF.Sqrt))
            sd_done = sE.val

            @block.vector
            def _(v, sl=sl, sd_done=sd_done):
                v.wait_ge(sE.sem, sd_done)
                vE.inc(v.reciprocal(cols[:, 7:8], cols[:, 6:7]))
                vE.inc(v.tensor_scalar(cols[:, 2:3], cols[:, 2:3], -1.0, None,
                                       ALU.mult))
                v.wait_ge(vE.sem, vE.val)
                vE.inc(v.tensor_scalar(Rt[:, sl], Rt[:, sl], cols[:, 2:3],
                                       cols[:, 7:8], ALU.add, ALU.mult))
            if out_dma is not None:
                out_dma(tt, vE.val)
        self.ln1_done_vE = vE.val

    def ph6_ffnprep(self, stk):
        nc, block = self.nc, self.block
        d = self.dram_in
        vE, sE, tM, dP, dW2, gI = (self.sem(x) for x in
                                   ["vE", "sE", "tM", "dP", "dW2", "gI"])
        self.XNT = self.sb("XNT", [128, 16 * OWN], BF16, stk)
        self.logits_sb = self.sb("logits_sb", [128, 8 * NE], F32, stk)
        self.rt = self.sb("rt", [128, 24], F32, stk)
        self.rte = self.sb("rte", [128, 4 * NE], F32, stk)
        self.idsu = self.sb("idsu", [128, 8], mybir.dt.uint32, stk)
        self.Mb = self.sb("Mb", [128, NE], BF16, stk)
        self.Mcum = self.sb("Mcum", [128, NE], BF16, stk)
        self.pos_sb = self.sb("pos_sb", [128, NE], F32, stk)
        self.idx_i = self.sb("idx_i", [128, 1], I32, stk)
        self.val_i = self.sb("val_i", [128, 1], I32, stk)

        ln_done = self.ln1_done_vE

        with ExitStack() as psc:
            self.xnorm_sb = self.sb("xnorm_sb", [128, 8 * C], BF16, psc)
            self.xnf = self.sb("xnf", [128, C], F32, psc)
            self.xntf = self.sb("xntf", [128, C], F32, psc)
            self.gatetF = self.sb("gatetF", [128, 16 * NE], F32, psc)
            self.normwB = self.sb("normwB", [128, C], BF16, psc)
            self.sq2 = self.sb("sq2", [128, C // 2], F32, psc)
            self.bigz = self.sb("bigz", [128, 48], I32, psc)
            self.bigz8 = self.sb("bigz8", [128, 48 * 8], I32, psc)
            self.zero2k = self.sb("zero2k", [128, C], BF16, psc)
            normw_row = self.sb("normw_row", [1, C], F32, psc)
            ones_f = self.sb("ones_f", [1, 128], F32, psc)
            ps_n = [self.psum(f"ps_n{i}", [128, 512], stack=psc) for i in range(4)]
            ps_l = self.psum("ps_l", [128, 512], stack=psc)
            ps_c = self.psum("ps_c", [128, 512], stack=psc)
            psb2 = self.psum("psb2", [128, 1024], BF16, stack=psc)

            @block.sync
            def _(sy):
                dP.dma(sy.dma_start(out=normw_row[:], in_=d["normw"]))
                dP.dma(sy.dma_start(
                    out=self.gatetF[:],
                    in_=d["gatet_s"].rearrange("p cc e -> p (cc e)")))
            nw_dP = dP.val

            @block.gpsimd
            def _(g):
                self.bcA = g.to_reg(NE * CAPL - 1)
                self.bcB = g.to_reg(TOPK * OWN - 1)
                gF = self.sem("gF")
                gF.inc(g.memset(self.bigz[:], 100000))
                gF.inc(g.memset(self.bigz8[:], 100000))
                gF.inc(g.memset(self.zero2k[:], 0.0))
                g.wait_ge(gF.sem, gF.val)
                gI.dma(g.dma_start(
                    out=self.IDINFO.ap().rearrange("(p q) o -> p (q o)", p=128),
                    in_=self.bigz8[:]))
                for kk in range(TOPK * OWN // 128):
                    gI.dma(g.dma_start(
                        out=self.CONTRIB[kk * 128:(kk + 1) * 128, :],
                        in_=self.zero2k[:]))
                for kk in range(NE * CAPL // 128):
                    gI.dma(g.dma_start(
                        out=self.GBUF[kk * 128:(kk + 1) * 128, :],
                        in_=self.zero2k[:]))
            self.contrib_zero_gI = gI.val

            @block.vector
            def _(v):
                vE.inc(v.memset(ones_f[:], 1.0))
            onesf_done = vE.val

            @block.tensor
            def _(t):
                t.wait_ge(dP.sem, nw_dP)
                t.wait_ge(vE.sem, onesf_done)
                t.wait_ge(self.sem("dW").sem, 32)
                for q in range(4):
                    tM.inc(t.matmul(ps_n[q][:], ones_f[:],
                                    normw_row[0:1, q * 512:(q + 1) * 512],
                                    start=True, stop=True))
            bc_tM = tM.val

            @block.vector
            def _(v):
                v.wait_ge(tM.sem, bc_tM)
                for q in range(4):
                    vE.inc(v.tensor_copy(self.normwB[:, q * 512:(q + 1) * 512],
                                         ps_n[q][:]))
            nwb_done = vE.val

            self.logit_ev = []
            self.xnf_free = []
            sq_free = 0
            cols = self.lncols
            for tt in range(8):
                sl = slice(tt * C, (tt + 1) * C)
                slA = slice(tt * C, tt * C + C // 2)
                slB = slice(tt * C + C // 2, (tt + 1) * C)

                @block.scalar
                def _(s, slA=slA, sq_free=sq_free):
                    s.wait_ge(vE.sem, max(ln_done, sq_free))
                    sE.inc(s.activation(self.sq2[:], self.R[:, slA], AF.Square))
                sqa = sE.val

                @block.vector
                def _(v, sqa=sqa):
                    v.wait_ge(sE.sem, sqa)
                    vE.inc(v.reduce_sum(cols[:, 0:1], self.sq2[:], axis=AX))
                ra = vE.val

                @block.scalar
                def _(s, slB=slB, ra=ra):
                    s.wait_ge(vE.sem, ra)
                    sE.inc(s.activation(self.sq2[:], self.R[:, slB], AF.Square))
                sqb = sE.val

                @block.vector
                def _(v, sqb=sqb):
                    v.wait_ge(sE.sem, sqb)
                    vE.inc(v.reduce_sum(cols[:, 1:2], self.sq2[:], axis=AX))
                    v.wait_ge(vE.sem, vE.val)
                    vE.inc(v.tensor_add(cols[:, 2:3], cols[:, 0:1], cols[:, 1:2]))
                    v.wait_ge(vE.sem, vE.val)
                    vE.inc(v.tensor_scalar(cols[:, 2:3], cols[:, 2:3], 1.0 / C,
                                           float(RMS_EPS), ALU.mult, ALU.add))
                sq_free = vE.val - 2
                ms_done = vE.val

                @block.scalar
                def _(s, ms_done=ms_done):
                    s.wait_ge(vE.sem, ms_done)
                    sE.inc(s.activation(cols[:, 3:4], cols[:, 2:3], AF.Sqrt))
                sd_done = sE.val

                @block.vector
                def _(v, sl=sl, tt=tt, sd_done=sd_done):
                    v.wait_ge(sE.sem, sd_done)
                    vE.inc(v.reciprocal(cols[:, 4:5], cols[:, 3:4]))
                    v.wait_ge(vE.sem, vE.val)
                    v.wait_ge(vE.sem, nwb_done)
                    if tt >= 1:
                        v.wait_ge(tM.sem, self.xnf_free[tt - 1])
                    vE.inc(v.scalar_tensor_tensor(
                        self.xnf[:], self.R[:, sl], cols[:, 4:5],
                        self.normwB[:], ALU.mult, ALU.mult))
                    v.wait_ge(vE.sem, vE.val)
                    vE.inc(v.tensor_copy(self.xnorm_sb[:, sl], self.xnf[:]))
                xn_done = vE.val

                trp_base = tM.val
                for half in range(2):
                    @block.tensor
                    def _(t, tt=tt, half=half, xn_done=xn_done,
                          trp_base=trp_base):
                        if half == 0:
                            t.wait_ge(vE.sem, xn_done)
                        else:
                            t.wait_ge(vE.sem, xn_done + 8)
                        ins = None
                        for q in range(8):
                            cc = half * 8 + q
                            ins = t.transpose(
                                psb2[:, q * 128:(q + 1) * 128],
                                self.xnorm_sb[:, tt * C + cc * 128:
                                              tt * C + (cc + 1) * 128],
                                self.ident[:])
                        tM.inc(ins)

                    @block.vector
                    def _(v, tt=tt, half=half, trp_base=trp_base):
                        v.wait_ge(tM.sem, trp_base + half + 1)
                        for q in range(8):
                            cc = half * 8 + q
                            vE.inc(v.tensor_copy(
                                self.XNT[:, cc * OWN + tt * 128:
                                         cc * OWN + (tt + 1) * 128],
                                psb2[:, q * 128:(q + 1) * 128]))
                xnt_done = vE.val

                trf_base = tM.val
                vef_base = vE.val

                @block.tensor
                def _(t, tt=tt, xn_done=xn_done, trf_base=trf_base,
                      vef_base=vef_base):
                    t.wait_ge(vE.sem, xn_done)
                    for cc in range(16):
                        if cc >= 4:
                            t.wait_ge(vE.sem, vef_base + cc - 3)
                        tM.inc(t.transpose(
                            ps_n[cc % 4][:, 0:128],
                            self.xnf[:, cc * 128:(cc + 1) * 128],
                            self.identF[:]))

                @block.vector
                def _(v, tt=tt, trf_base=trf_base):
                    for cc in range(16):
                        v.wait_ge(tM.sem, trf_base + cc + 1)
                        vE.inc(v.tensor_copy(
                            self.xntf[:, cc * 128:(cc + 1) * 128],
                            ps_n[cc % 4][:, 0:128]))
                xntf_done = vE.val

                @block.tensor
                def _(t, tt=tt, xntf_done=xntf_done):
                    t.wait_ge(vE.sem, xntf_done)
                    if tt >= 1:
                        t.wait_ge(vE.sem, self.logit_ev[tt - 1])
                    ins = None
                    for cc in range(16):
                        ins = t.matmul(ps_l[:, 0:NE],
                                       self.xntf[:, cc * 128:(cc + 1) * 128],
                                       self.gatetF[:, cc * NE:(cc + 1) * NE],
                                       start=(cc == 0), stop=(cc == 15))
                    tM.inc(ins)
                self.xnf_free.append(tM.val)
                lg_mm = tM.val

                @block.vector
                def _(v, tt=tt, lg_mm=lg_mm):
                    v.wait_ge(tM.sem, lg_mm)
                    vE.inc(v.tensor_copy(
                        self.logits_sb[:, tt * NE:(tt + 1) * NE], ps_l[:, 0:NE]))
                self.logit_ev.append(vE.val)

            for tt in range(8):
                self._route_tile(tt, ps_c, gI)
            self.route_done_vE = vE.val
            self.route_done_gI = gI.val

            if "xnorm" in self.tap_out:
                stO = self.sem("stO")

                @block.sync
                def _(sy):
                    sy.wait_ge(vE.sem, vE.val)
                    stO.dma(sy.dma_start(out=self.TAPD[:, :8 * C],
                                         in_=self.xnorm_sb[:]))

                @block.gpsimd
                def _(g):
                    g.wait_ge(stO.sem, stO.val)
                    stO.dma(g.dma_start(
                        out=self.tap_out["xnorm"].rearrange(
                            "(tt p) c -> p tt c", p=128),
                        in_=self.TAPD[:, :8 * C].rearrange(
                            "p (tt c) -> p tt c", tt=8)))
        self.xn_ready_vE = vE.val

    def _route_tile(self, tt, ps_c, gI):
        nc, block = self.nc, self.block
        vE, sE, tM = (self.sem(x) for x in ["vE", "sE", "tM"])
        cols = self.rt
        lg = self.logits_sb[:, tt * NE:(tt + 1) * NE]
        gI_base = gI.val

        @block.vector
        def _(v):
            v.wait_ge(vE.sem, self.logit_ev[tt])
            vE.inc(v.max(out=cols[:, 0:8], in_=lg))
            v.wait_ge(vE.sem, vE.val)
            vE.inc(v.max_index(out=self.idsu[:], in_max=cols[:, 0:8],
                               in_values=lg))
            v.wait_ge(vE.sem, vE.val)
            vE.inc(v.tensor_copy(cols[:, 12:16], self.idsu[:, 0:4]))
            vE.inc(v.tensor_scalar(cols[:, 16:20], cols[:, 0:4], cols[:, 0:1],
                                   None, ALU.subtract))
        sm_done = vE.val

        @block.scalar
        def _(s):
            s.wait_ge(vE.sem, sm_done)
            sE.inc(s.activation(cols[:, 16:20], cols[:, 16:20], AF.Exp))
        ex_done = sE.val

        @block.vector
        def _(v):
            v.wait_ge(sE.sem, ex_done)
            vE.inc(v.reduce_sum(cols[:, 20:21], cols[:, 16:20], axis=AX))
            v.wait_ge(vE.sem, vE.val)
            vE.inc(v.reciprocal(cols[:, 21:22], cols[:, 20:21]))
            v.wait_ge(vE.sem, vE.val)
            vE.inc(v.tensor_scalar(self.wids[:, tt * 8 + 4:tt * 8 + 8],
                                   cols[:, 16:20], cols[:, 21:22], None,
                                   ALU.mult))
            vE.inc(v.tensor_copy(self.wids[:, tt * 8:tt * 8 + 4],
                                 cols[:, 12:16]))
            v.wait_ge(vE.sem, vE.val)
            for k in range(4):
                vE.inc(v.tensor_tensor(
                    self.rte[:, k * NE:(k + 1) * NE],
                    cols[:, 12 + k:13 + k].to_broadcast([128, NE]),
                    self.iota32[:], ALU.is_equal))
            v.wait_ge(vE.sem, vE.val)
            vE.inc(v.tensor_add(self.pos_sb[:], self.rte[:, 0:NE],
                                self.rte[:, NE:2 * NE]))
            vE.inc(v.tensor_add(self.Mb[:], self.rte[:, 2 * NE:3 * NE],
                                self.rte[:, 3 * NE:4 * NE]))
            v.wait_ge(vE.sem, vE.val)
            vE.inc(v.tensor_add(self.Mb[:], self.Mb[:], self.pos_sb[:]))
        m_done = vE.val

        @block.tensor
        def _(t):
            t.wait_ge(vE.sem, m_done)
            if tt == 0:
                ins = t.matmul(ps_c[:, 0:NE], self.cumL[:], self.Mb[:],
                               start=True, stop=True)
            else:
                t.matmul(ps_c[:, 0:NE], self.cumL[:], self.Mb[:],
                         start=True, stop=False)
                ins = t.matmul(ps_c[:, 0:NE], self.ones_sb[:], self.Mcum[:],
                               start=False, stop=True)
            tM.inc(ins)
        cs_done = tM.val

        @block.vector
        def _(v):
            v.wait_ge(tM.sem, cs_done)
            vE.inc(v.tensor_copy(self.pos_sb[:], ps_c[:, 0:NE]))
            if tt == 0:
                vE.inc(v.tensor_copy(self.Mcum[:], self.Mb[:]))
            else:
                vE.inc(v.tensor_add(self.Mcum[:], self.Mcum[:], self.Mb[:]))
        pos_done = vE.val

        for k in range(4):
            @block.vector
            def _(v, k=k):
                v.wait_ge(vE.sem, pos_done)
                v.wait_ge(gI.sem, gI_base + 32 * k if (tt or k) else 0)
                vE.inc(v.tensor_mul(self.rte[:, k * NE:(k + 1) * NE],
                                    self.rte[:, k * NE:(k + 1) * NE],
                                    self.pos_sb[:]))
                v.wait_ge(vE.sem, vE.val)
                vE.inc(v.reduce_sum(cols[:, 22:23],
                                    self.rte[:, k * NE:(k + 1) * NE], axis=AX))
                v.wait_ge(vE.sem, vE.val)
                vE.inc(v.scalar_tensor_tensor(cols[:, 23:24],
                                              cols[:, 12 + k:13 + k],
                                              float(CAPL), cols[:, 22:23],
                                              ALU.mult, ALU.add))
                vE.inc(v.tensor_scalar(cols[:, 20:21], cols[:, 22:23],
                                       float(CAPL), None, ALU.is_ge))
                v.wait_ge(vE.sem, vE.val)
                vE.inc(v.scalar_tensor_tensor(cols[:, 23:24], cols[:, 20:21],
                                              1.0e5, cols[:, 23:24],
                                              ALU.mult, ALU.add))
                vE.inc(v.tensor_scalar(cols[:, 22:23], self.iotap[:],
                                       float(k * OWN + tt * 128), None,
                                       ALU.add))
                v.wait_ge(vE.sem, vE.val)
                vE.inc(v.tensor_copy(self.idx_i[:], cols[:, 23:24]))
                vE.inc(v.tensor_copy(self.val_i[:], cols[:, 22:23]))
            sc_ready = vE.val

            @block.gpsimd
            def _(g, k=k, tt=tt, sc_ready=sc_ready):
                g.wait_ge(vE.sem, sc_ready)
                if tt == 0 and k == 0:
                    g.wait_ge(gI.sem, self.contrib_zero_gI)
                gI.dma(g.indirect_dma_start(
                    out=self.GBUF.ap(),
                    out_offset=IndirectOffsetOnAxis(self.idx_i[:, 0:1], 0),
                    in_=self.xnorm_sb[:, tt * C:(tt + 1) * C], in_offset=None,
                    bounds_check=self.bcA, oob_is_err=False))
                gI.dma(g.indirect_dma_start(
                    out=self.IDINFO.ap(),
                    out_offset=IndirectOffsetOnAxis(self.idx_i[:, 0:1], 0),
                    in_=self.val_i[:], in_offset=None,
                    bounds_check=self.bcA, oob_is_err=False))

    def ph7_shared(self, stk):
        nc, block = self.nc, self.block
        d = self.dram_in
        vE, sE, tM, dP, dW2 = (self.sem(x) for x in
                               ["vE", "sE", "tM", "dP", "dW2"])
        with ExitStack() as sub:
            sdw_sb = self.sb("sdw_sb", [128, 8 * C], BF16, sub)
            act_sb = self.sb("act_sb", [128, 8 * OWN], BF16, sub)
            wg = [self.sb(f"wg{i}", [128, C], BF16, sub) for i in range(2)]
            wu = [self.sb(f"wu{i}", [128, C], BF16, sub) for i in range(2)]
            silu_sb = self.sb("silu_sb", [128, 512], BF16, sub)
            shst = [self.sb(f"shst{i}", [128, C], BF16, sub) for i in range(2)]
            ps_g = [self.psum(f"ps_g{i}", [128, 512], stack=sub) for i in range(2)]
            ps_u = [self.psum(f"ps_u{i}", [128, 512], stack=sub) for i in range(2)]
            ps_d = self.psum("ps_d", [128, 2048], stack=sub)

            dw0 = dW2.val
            tm0 = tM.val
            xn_need = self.xn_ready_vE
            pp = [self.sem("pA"), self.sem("pB")]
            qq = [self.sem("qA"), self.sem("qB")]
            rr = [self.sem("rA"), self.sem("rB")]
            pp0 = [p.val for p in pp]
            qq0 = [q.val for q in qq]
            rr0 = [r.val for r in rr]

            @block.gpsimd
            def _(g):
                g.wait_ge(self.sem("gI").sem, self.route_done_gI)
                for ic in range(8):
                    dW2.dma(g.dma_start(out=sdw_sb[:, ic * C:(ic + 1) * C],
                                        in_=d["sdw"][ic * 128:(ic + 1) * 128, :]))
                for ic in range(8):
                    g.wait_ge(tM.sem, max(0, tm0 + ic - 1))
                    pp[ic % 2].dma(g.dma_start(
                        out=wg[ic % 2][:],
                        in_=d["sgw_s"][ic].rearrange("p cc pi -> p (cc pi)")))
                    qq[ic % 2].dma(g.dma_start(
                        out=wu[ic % 2][:],
                        in_=d["suw_s"][ic].rearrange("p cc pi -> p (cc pi)")))

            for ic in range(8):
                vec = vE.val

                @block.tensor
                def _(t, ic=ic, vec=vec):
                    t.wait_ge(pp[ic % 2].sem, pp0[ic % 2] + 16 * (ic // 2 + 1))
                    t.wait_ge(qq[ic % 2].sem, qq0[ic % 2] + 16 * (ic // 2 + 1))
                    t.wait_ge(vE.sem, xn_need)
                    if ic >= 1:
                        t.wait_ge(vE.sem, vec)
                    ins = None
                    for half in range(2):
                        for cc in range(16):
                            t.matmul(ps_g[half][:],
                                     wg[ic % 2][:, cc * 128:(cc + 1) * 128],
                                     self.XNT[:, cc * OWN + half * 512:
                                              cc * OWN + (half + 1) * 512],
                                     start=(cc == 0), stop=(cc == 15))
                            ins = t.matmul(
                                ps_u[half][:],
                                wu[ic % 2][:, cc * 128:(cc + 1) * 128],
                                self.XNT[:, cc * OWN + half * 512:
                                         cc * OWN + (half + 1) * 512],
                                start=(cc == 0), stop=(cc == 15))
                    tM.inc(ins)
                gu_done = tM.val

                se_b = sE.val

                @block.scalar
                def _(s, gu_done=gu_done, vec=vec):
                    for half in range(2):
                        s.wait_ge(tM.sem, gu_done)
                        if half == 1:
                            s.wait_ge(vE.sem, vec + 1)
                        sE.inc(s.activation(silu_sb[:], ps_g[half][:], AF.Silu))

                @block.vector
                def _(v, ic=ic, se_b=se_b):
                    for half in range(2):
                        v.wait_ge(sE.sem, se_b + half + 1)
                        vE.inc(v.tensor_mul(
                            act_sb[:, ic * OWN + half * 512:
                                   ic * OWN + (half + 1) * 512],
                            silu_sb[:], ps_u[half][:]))
            act_done_vE = vE.val

            ev_list = []
            dn0 = dP.val
            for tt in range(8):
                @block.tensor
                def _(t, tt=tt):
                    t.wait_ge(vE.sem, act_done_vE)
                    t.wait_ge(dW2.sem, dw0 + 16 * 8)
                    if tt >= 1:
                        t.wait_ge(vE.sem, ev_list[tt - 1])
                    ins = None
                    for ic in range(8):
                        st = act_sb[:, ic * OWN + tt * 128:
                                    ic * OWN + (tt + 1) * 128]
                        for q in range(4):
                            ins = t.matmul(ps_d[:, q * 512:(q + 1) * 512], st,
                                           sdw_sb[:, ic * C + q * 512:
                                                  ic * C + (q + 1) * 512],
                                           start=(ic == 0), stop=(ic == 7))
                    tM.inc(ins)
                dn_done = tM.val

                @block.vector
                def _(v, tt=tt, dn_done=dn_done):
                    v.wait_ge(tM.sem, dn_done)
                    if tt >= 2:
                        v.wait_ge(rr[tt % 2].sem, rr0[tt % 2] + 16 * (tt // 2))
                    vE.inc(v.tensor_copy(shst[tt % 2][:], ps_d[:]))
                ev_list.append(vE.val)

                @block.sync
                def _(sy, tt=tt, ev=ev_list[tt]):
                    sy.wait_ge(vE.sem, ev)
                    rr[tt % 2].dma(sy.dma_start(
                        out=self.SHARED[tt * 128:(tt + 1) * 128, :],
                        in_=shst[tt % 2][:]))
            self.shared_done_st = (rr[0].val, rr[1].val)
            self.shared_done_dP = dP.val

    def ph8_spill_r(self, stk):
        nc, block = self.nc, self.block
        dP, vE = self.sem("dP"), self.sem("vE")

        @block.sync
        def _(sy):
            sy.wait_ge(vE.sem, self.xn_ready_vE)
            dP.dma(sy.dma_start(
                out=self.RSPILL.ap().rearrange("(tt p) c -> p tt c", p=128),
                in_=self.R[:].rearrange("p (tt c) -> p tt c", tt=8)))
        self.rspill_done_dP = dP.val

    def ph9_moe(self, stk):
        nc, block = self.nc, self.block
        d = self.dram_in
        vE, sE, tM, dW2, dT, gI = (self.sem(x) for x in
                                   ["vE", "sE", "tM", "dW2", "dT", "gI"])
        GR = 4
        GW = GR * CAPL
        w13buf = [self.sb(f"w13b{i}", [128, 16 * NI], BF16, stk)
                  for i in range(2)]
        w2buf = [self.sb(f"w2b{i}", [128, 8 * C], BF16, stk) for i in range(2)]
        gbufT = [self.sb(f"gbT{i}", [128, 16 * GW], BF16, stk)
                 for i in range(2)]
        g_sb = self.sb("g_sb", [128, 8 * CAPL], BF16, stk)
        act_sb = self.sb("act_sb2", [128, 8 * CAPL], BF16, stk)
        silu2 = self.sb("silu2", [128, CAPL], BF16, stk)
        outst = [self.sb(f"outst{i}", [128, C], BF16, stk) for i in range(2)]
        w2stg = [self.sb(f"w2stg{i}", [128, 512], F32, stk) for i in range(4)]
        idxa = [self.sb(f"idxa{i}", [128, 1], I32, stk) for i in range(2)]
        idxb = [self.sb(f"idxb{i}", [64, 1], I32, stk) for i in range(2)]
        ps_w = [self.psum(f"ps_w{i}", [128, 512], stack=stk) for i in range(3)]
        ps_o2 = self.psum("ps_o2", [128, 2048], stack=stk)

        u_done = [0] * NE
        g_done = [0] * NE
        w2_done = [0] * NE
        mu_done = [0] * NE
        w2ev = [0] * NE
        sc_done = [0] * NE
        self.moe_pp = pp = [self.sem("pA"), self.sem("pB")]
        wW = [self.sem("wA2"), self.sem("wB2"), self.sem("wC2"), self.sem("wD2")]
        wW0 = [w.val for w in wW]
        cW = self.sem("cW")
        cb0 = cW.val
        self.moe_qq = qq = [self.sem("qA"), self.sem("qB")]
        self.moe_tt = tt_s = [self.sem("tA"), self.sem("tB")]
        self.moe_pp0 = pp0 = [p.val for p in pp]
        self.moe_qq0 = qq0 = [q.val for q in qq]
        self.moe_tt0 = tt0 = [t.val for t in tt_s]
        self.moe_dT0 = dT.val

        for e in range(NE):
            grp, eg = e // GR, e % GR

            if eg == 0:
                @block.sync
                def _(sc, grp=grp):
                    if grp == 0:
                        sc.wait_ge(gI.sem, self.route_done_gI)
                        sc.wait_ge(self.sem("dP").sem, self.rspill_done_dP)
                        sc.wait_ge(self.sem("rA").sem, self.shared_done_st[0])
                        sc.wait_ge(self.sem("rB").sem, self.shared_done_st[1])
                    if grp >= 2:
                        sc.wait_ge(tM.sem, u_done[grp * GR - GR - 1])
                    for cc in range(16):
                        tt_s[grp % 2].dma(sc.dma_start_transpose(
                            gbufT[grp % 2][:, cc * GW:(cc + 1) * GW],
                            self.GBUF[grp * GW:(grp + 1) * GW,
                                      cc * 128:(cc + 1) * 128]))
            dT_need = (grp % 2, tt0[grp % 2] + 256 * (grp // 2 + 1))

            dwb = dW2.val

            @block.gpsimd
            def _(g, e=e):
                if e == 0:
                    g.wait_ge(gI.sem, self.contrib_zero_gI)
                    g.wait_ge(self.sem("dP").sem, self.rspill_done_dP)
                    g.wait_ge(self.sem("rA").sem, self.shared_done_st[0])
                    g.wait_ge(self.sem("rB").sem, self.shared_done_st[1])
                if e >= 1:
                    g.wait_ge(tM.sem, g_done[e - 1])
                for cc in range(16):
                    pp[0].dma(g.dma_start(
                        out=w13buf[0][:, cc * NI:(cc + 1) * NI],
                        in_=d["w13"][e, cc * 128:(cc + 1) * 128, 0:NI]))
                if e >= 1:
                    g.wait_ge(tM.sem, u_done[e - 1])
                for cc in range(16):
                    pp[1].dma(g.dma_start(
                        out=w13buf[1][:, cc * NI:(cc + 1) * NI],
                        in_=d["w13"][e, cc * 128:(cc + 1) * 128, NI:2 * NI]))
                gI.dma(g.dma_start(
                    out=idxa[e % 2][:],
                    in_=self.IDINFO[e * CAPL:e * CAPL + 128, 0:1],
                    allow_slow_non_contiguous=True))
                gI.dma(g.dma_start(
                    out=idxb[e % 2][:],
                    in_=self.IDINFO[e * CAPL + 128:(e + 1) * CAPL, 0:1],
                    allow_slow_non_contiguous=True))
            dw_h0 = pp0[0] + 16 * 16 * (e + 1)
            dw_h1 = pp0[1] + 16 * 16 * (e + 1)
            idx_gI = gI.val

            @block.sync
            def _(sy, e=e):
                for k in range(32):
                    m = e * 32 + k
                    if m >= 4:
                        sy.wait_ge(cW.sem, cb0 + m - 3)
                    wW[m % 4].dma(sy.dma_start(
                        out=w2stg[m % 4][:],
                        in_=d["w2"][e, (k // 4) * 128:(k // 4 + 1) * 128,
                                    (k % 4) * 512:(k % 4 + 1) * 512]))

            @block.vector
            def _(v, e=e):
                if e >= 2:
                    v.wait_ge(tM.sem, w2_done[e - 2])
                    v.wait_ge(self.sem("gS").sem, sc_done[e - 2])
                for k in range(32):
                    m = e * 32 + k
                    v.wait_ge(wW[m % 4].sem, wW0[m % 4] + 16 * (m // 4 + 1))
                    cW.inc(v.tensor_copy(
                        w2buf[e % 2][:, k * 512:(k + 1) * 512],
                        w2stg[m % 4][:]))

            vg_base = vE.val

            @block.tensor
            def _(t, e=e, grp=grp, eg=eg, dw_h0=dw_h0, dT_need=dT_need,
                  vg_base=vg_base):
                t.wait_ge(pp[0].sem, dw_h0)
                t.wait_ge(tt_s[dT_need[0]].sem, dT_need[1])
                if e >= 1:
                    t.wait_ge(vE.sem, mu_done[e - 1])
                for fc in range(8):
                    if fc >= 3:
                        t.wait_ge(vE.sem, vg_base + fc - 2)
                    ins = None
                    for cc in range(16):
                        ins = t.matmul(
                            ps_w[fc % 3][:, 0:CAPL],
                            w13buf[0][:, cc * NI + fc * 128:
                                      cc * NI + (fc + 1) * 128],
                            gbufT[grp % 2][:, cc * GW + eg * CAPL:
                                           cc * GW + (eg + 1) * CAPL],
                            start=(cc == 0), stop=(cc == 15))
                    tM.inc(ins)
            g_done[e] = tM.val
            gm_base = g_done[e] - 8

            @block.vector
            def _(v, gm_base=gm_base):
                for fc in range(8):
                    v.wait_ge(tM.sem, gm_base + fc + 1)
                    vE.inc(v.tensor_copy(g_sb[:, fc * CAPL:(fc + 1) * CAPL],
                                         ps_w[fc % 3][:, 0:CAPL]))
            assert vE.val == vg_base + 8

            vu_base = vE.val

            @block.tensor
            def _(t, e=e, grp=grp, eg=eg, dw_h1=dw_h1, vu_base=vu_base,
                  vg_base=vg_base):
                t.wait_ge(pp[1].sem, dw_h1)
                t.wait_ge(vE.sem, vg_base + 8)
                for fc in range(8):
                    if fc >= 3:
                        t.wait_ge(vE.sem, vu_base + fc - 2)
                    ins = None
                    for cc in range(16):
                        ins = t.matmul(
                            ps_w[fc % 3][:, 0:CAPL],
                            w13buf[1][:, cc * NI + fc * 128:
                                      cc * NI + (fc + 1) * 128],
                            gbufT[grp % 2][:, cc * GW + eg * CAPL:
                                           cc * GW + (eg + 1) * CAPL],
                            start=(cc == 0), stop=(cc == 15))
                    tM.inc(ins)
            u_done[e] = tM.val
            um_base = u_done[e] - 8

            sE_base = sE.val

            @block.scalar
            def _(sc, um_base=um_base, vu_base=vu_base):
                for fc in range(8):
                    sc.wait_ge(tM.sem, um_base + fc + 1)
                    if fc >= 1:
                        sc.wait_ge(vE.sem, vu_base + fc)
                    sE.inc(sc.activation(silu2[:, 0:CAPL],
                                         g_sb[:, fc * CAPL:(fc + 1) * CAPL],
                                         AF.Silu))

            @block.vector
            def _(v, sE_base=sE_base):
                for fc in range(8):
                    v.wait_ge(sE.sem, sE_base + fc + 1)
                    vE.inc(v.tensor_mul(act_sb[:, fc * CAPL:(fc + 1) * CAPL],
                                        silu2[:, 0:CAPL],
                                        ps_w[fc % 3][:, 0:CAPL]))
            mu_done[e] = vE.val
            assert mu_done[e] == vu_base + 8

            w2ev_base = vE.val

            @block.tensor
            def _(t, e=e, w2ev_base=w2ev_base):
                t.wait_ge(cW.sem, cb0 + 32 * (e + 1))
                t.wait_ge(vE.sem, mu_done[e])
                if e >= 1:
                    t.wait_ge(vE.sem, w2ev[e - 1])
                ins = None
                for sc in range(2):
                    w = 128 if sc == 0 else CAPL - 128
                    if sc == 1:
                        t.wait_ge(vE.sem, w2ev_base + 1)
                    for ic in range(8):
                        st = act_sb[:, ic * CAPL + sc * 128:
                                    ic * CAPL + sc * 128 + w]
                        for q in range(4):
                            ins = t.matmul(
                                ps_o2[0:w, q * 512:(q + 1) * 512], st,
                                w2buf[e % 2][:, ic * C + q * 512:
                                             ic * C + (q + 1) * 512],
                                start=(ic == 0), stop=(ic == 7))
                    tM.inc(ins)
            w2_done[e] = tM.val

            for sc in range(2):
                w = 128 if sc == 0 else CAPL - 128

                @block.vector
                def _(v, e=e, sc=sc, w=w):
                    v.wait_ge(tM.sem, w2_done[e] - 1 + sc)
                    if e >= 1:
                        v.wait_ge(self.sem("gS").sem, sc_done[e - 1])
                    vE.inc(v.tensor_copy(outst[sc][0:w, :], ps_o2[0:w, :]))
            w2ev[e] = vE.val

            gS = self.sem("gS")

            @block.gpsimd
            def _(g, e=e, idx_gI=idx_gI, w2ev_e=w2ev[e]):
                g.wait_ge(gI.sem, idx_gI)
                g.wait_ge(vE.sem, w2ev_e - 1)
                gS.dma(g.indirect_dma_start(
                    out=self.CONTRIB.ap(),
                    out_offset=IndirectOffsetOnAxis(idxa[e % 2][:, 0:1], 0),
                    in_=outst[0][0:128, :], in_offset=None,
                    bounds_check=self.bcB, oob_is_err=False))
                g.wait_ge(vE.sem, w2ev_e)
                gS.dma(g.indirect_dma_start(
                    out=self.CONTRIB.ap(),
                    out_offset=IndirectOffsetOnAxis(idxb[e % 2][:, 0:1], 0),
                    in_=outst[1][0:CAPL - 128, :], in_offset=None,
                    bounds_check=self.bcB, oob_is_err=False))
            sc_done[e] = gS.val
        self.moe_done_gI = gS.val

    def ph10_return(self, stk):
        nc, block = self.nc, self.block
        vE, sE, dP, gI, stO = (self.sem(x) for x in
                               ["vE", "sE", "dP", "gI", "stO"])
        acc = [self.sb(f"acc{i}", [128, C], F32, stk) for i in range(2)]
        rl = self.sb("rl", [128, C], F32, stk)
        sh = self.sb("sh", [128, C], BF16, stk)
        cb = [self.sb(f"cb{i}", [128, C], BF16, stk) for i in range(4)]
        self.sq_sb = self.sb("sq3", [128, C // 2], F32, stk)
        self.ret_done = []

        for tt in range(8):
            ld0 = dP.val

            @block.sync
            def _(sy, tt=tt):
                sy.wait_ge(self.sem("gS").sem, self.moe_done_gI)
                sy.wait_ge(dP.sem, self.rspill_done_dP)
                sy.wait_ge(self.sem("rA").sem, self.shared_done_st[0])
                sy.wait_ge(self.sem("rB").sem, self.shared_done_st[1])
                if tt >= 1:
                    sy.wait_ge(vE.sem, self.ret_done[tt - 1])
                dP.dma(sy.dma_start(out=rl[:],
                                    in_=self.RSPILL[tt * 128:(tt + 1) * 128, :]))
                dP.dma(sy.dma_start(out=sh[:],
                                    in_=self.SHARED[tt * 128:(tt + 1) * 128, :]))
                for k in range(4):
                    dP.dma(sy.dma_start(
                        out=cb[k][:],
                        in_=self.CONTRIB[k * OWN + tt * 128:
                                         k * OWN + (tt + 1) * 128, :]))
            ld_done = dP.val

            @block.vector
            def _(v, tt=tt, ld_done=ld_done, st_prev=stO.val):
                v.wait_ge(dP.sem, ld_done)
                v.wait_ge(stO.sem, st_prev)
                vE.inc(v.scalar_tensor_tensor(acc[0][:], rl[:], 2.0, sh[:],
                                              ALU.mult, ALU.add))
                cur = 0
                for k in range(4):
                    v.wait_ge(vE.sem, vE.val)
                    vE.inc(v.scalar_tensor_tensor(
                        acc[1 - cur][:], cb[k][:],
                        self.wids[:, tt * 8 + 4 + k:tt * 8 + 5 + k],
                        acc[cur][:], ALU.mult, ALU.add))
                    cur = 1 - cur
                self.ret_cur = cur

            cur = self.ret_cur

            def store(dummy_tt, ln_vE, tt=tt, cur=cur):
                @block.sync
                def _(sy):
                    sy.wait_ge(vE.sem, ln_vE)
                    stO.dma(sy.dma_start(
                        out=self.orows[tt * 128:(tt + 1) * 128, :],
                        in_=acc[cur][:]))

            self._layernorm_tiles(acc[cur], [0], LN_EPS, out_dma=store)
            self.ret_done.append(vE.val)

    def emit_taps_prep(self):
        self.tap_bounce([("qT", self.qT_sb, 16 * OWN), ("keff", self.keff_sb, 16 * NL)]
                        + [(f"veff{lc}", self.veff_sb[lc], C) for lc in range(4)])

    def tap_bounce(self, items, pre_dP=None, pre_vE=None):
        nc, block = self.nc, self.block
        stO, vE, dP = (self.sem(s) for s in ["stO", "vE", "dP"])
        tap = self.tap_out

        for name, src, width in items:
            base = name.rstrip("0123456789")
            if base not in tap and name not in tap:
                continue
            tname = name if name in tap else base
            dst = tap[tname]
            if base != name:
                dst = dst[int(name[len(base):])]

            @block.sync
            def _(sy, src=src, width=width):
                sy.wait_ge(vE.sem, vE.val)
                sy.wait_ge(self.sem("tM").sem, self.sem("tM").val)
                sy.wait_ge(stO.sem, stO.val)
                dP.dma(sy.dma_start(out=self.TAPD[:, :width], in_=src[:]))

            @block.gpsimd
            def _(g, dst=dst, width=width):
                g.wait_ge(dP.sem, dP.val)
                stO.dma(g.dma_start(out=dst[:, :] if len(dst.shape) == 2 else dst,
                                    in_=self.TAPD[:, :width]))
        if "ckv" in tap:
            @block.gpsimd
            def _(g):
                g.wait_ge(dP.sem, self.ckv_done_dP)
                stO.dma(g.dma_start(out=tap["ckv"][:, :], in_=self.CKV[:, :]))

    def emit_taps_attn(self, wait_vE):
        nc, block = self.nc, self.block
        stO, vE = self.sem("stO"), self.sem("vE")
        if "attn_r" not in self.tap_out:
            return

        @block.sync
        def _(sy):
            sy.wait_ge(vE.sem, wait_vE)
            stO.dma(sy.dma_start(
                out=self.tap_out["attn_r"].rearrange("(tt p) c -> p tt c", p=128),
                in_=self.R[:].rearrange("p (tt c) -> p tt c", tt=8)))

    def emit_taps_ffnprep(self):
        nc, block = self.nc, self.block
        stO, vE, dP, gI = (self.sem(x) for x in ["stO", "vE", "dP", "gI"])
        tap = self.tap_out
        items = [("xnorm8", None, None)]

        if "logits" in tap:
            @block.sync
            def _(sy):
                sy.wait_ge(vE.sem, self.xn_ready_vE)
                stO.dma(sy.dma_start(
                    out=tap["logits"].rearrange("(tt p) e -> p tt e", p=128),
                    in_=self.logits_sb[:].rearrange("p (tt e) -> p tt e", tt=8)))

        if "routew" in tap:
            @block.sync
            def _(sy):
                sy.wait_ge(vE.sem, self.xn_ready_vE)
                stO.dma(sy.dma_start(
                    out=tap["routew"].rearrange("(tt p) k -> p tt k", p=128),
                    in_=self.wids[:].rearrange(
                        "p (tt x) -> p tt x", x=8)[:, :, 4:8]))
                stO.dma(sy.dma_start(
                    out=tap["routeid"].rearrange("(tt p) k -> p tt k", p=128),
                    in_=self.wids[:].rearrange(
                        "p (tt x) -> p tt x", x=8)[:, :, 0:4]))

        if "gbuf" in tap:
            @block.gpsimd
            def _(g):
                g.wait_ge(gI.sem, self.route_done_gI)
                stO.dma(g.dma_start(out=tap["gbuf"][:, :], in_=self.GBUF[:, :]))
                stO.dma(g.dma_start(out=tap["idinfo"][:, :],
                                    in_=self.IDINFO[:, :]))
        if "rsp" in tap:
            @block.gpsimd
            def _(g):
                g.wait_ge(self.sem("dP").sem, self.rspill_done_dP)
                stO.dma(g.dma_start(out=tap["rsp"][:, :], in_=self.RSPILL[:, :]))
        if "shared" in tap:
            @block.gpsimd
            def _(g):
                g.wait_ge(self.sem("dP").sem, self.shared_done_dP)
                stO.dma(g.dma_start(out=tap["shared"][:, :],
                                    in_=self.SHARED[:, :]))

        if "shared" in tap:
            @block.gpsimd
            def _(g):
                g.wait_ge(self.sem("rA").sem, self.shared_done_st[0])
                g.wait_ge(self.sem("rB").sem, self.shared_done_st[1])
                stO.dma(g.dma_start(out=tap["shared"][:, :],
                                    in_=self.SHARED[:, :]))

    def emit_taps_full(self):
        nc, block = self.nc, self.block
        stO, gI = self.sem("stO"), self.sem("gI")
        tap = self.tap_out
        if "contrib" in tap:
            @block.gpsimd
            def _(g):
                g.wait_ge(self.sem("gS").sem, self.moe_done_gI)
                stO.dma(g.dma_start(out=tap["contrib"][:, :],
                                    in_=self.CONTRIB[:, :]))
        if "gbuf" in tap:
            @block.gpsimd
            def _(g):
                g.wait_ge(gI.sem, self.route_done_gI)
                stO.dma(g.dma_start(out=tap["gbuf"][:, :], in_=self.GBUF[:, :]))
                stO.dma(g.dma_start(out=tap["idinfo"][:, :],
                                    in_=self.IDINFO[:, :]))
        if "rsp" in tap:
            @block.gpsimd
            def _(g):
                g.wait_ge(self.sem("dP").sem, self.rspill_done_dP)
                stO.dma(g.dma_start(out=tap["rsp"][:, :], in_=self.RSPILL[:, :]))
        if "shared" in tap:
            @block.gpsimd
            def _(g):
                g.wait_ge(self.sem("dP").sem, self.shared_done_dP)
                stO.dma(g.dma_start(out=tap["shared"][:, :],
                                    in_=self.SHARED[:, :]))

    def finish(self):
        block = self.block
        stO = self.sem("stO")

        @block.sync
        def _(sy):
            sy.wait_ge(stO.sem, stO.val)


def host_layouts(L0_attn_x, L0_attn_Wq_w, L0_attn_Wdkv_w, L0_attn_Wuk_w,
                 L0_attn_Wuv_w, L0_attn_Wo_w, L0_ffn_norm_w, L0_ffn_gate_w,
                 L0_ffn_w13, L0_ffn_w2, L0_ffn_shared_gate_w,
                 L0_ffn_shared_up_w, L0_ffn_shared_down_w):
    xflat = np.ascontiguousarray(np.asarray(L0_attn_x, np.float32).reshape(NT, C))
    Wq = np.asarray(L0_attn_Wq_w, np.float32)
    shared = {
        "xt_tiled": np.ascontiguousarray(
            xflat.reshape(NT // 128, 128, 16, 128).transpose(0, 3, 2, 1)),
        "wq_s": np.ascontiguousarray(
            Wq.reshape(16, 128, 16, 128).transpose(2, 1, 0, 3)),
        "wqt_s": np.ascontiguousarray(
            Wq.reshape(16, 128, 16, 128).transpose(0, 3, 2, 1)),
        "wdkvt_s": np.ascontiguousarray(
            np.asarray(L0_attn_Wdkv_w, np.float32).T.reshape(16, 128, NL)
            .transpose(1, 0, 2)),
        "wuk_s": np.ascontiguousarray(
            np.asarray(L0_attn_Wuk_w, np.float32).reshape(16, 128, NL)
            .transpose(1, 0, 2)),
        "wuv_s": np.ascontiguousarray(
            np.asarray(L0_attn_Wuv_w, np.float32).reshape(16, 128, NL)
            .transpose(1, 0, 2)),
        "wot": np.ascontiguousarray(np.asarray(L0_attn_Wo_w, np.float32).T),
        "gatet_s": np.ascontiguousarray(
            np.asarray(L0_ffn_gate_w, np.float32).T.reshape(16, 128, NE)
            .transpose(1, 0, 2)),
        "normw": np.ascontiguousarray(
            np.asarray(L0_ffn_norm_w, np.float32).reshape(1, C)),
        "sgw_s": np.ascontiguousarray(
            np.asarray(L0_ffn_shared_gate_w, np.float32)
            .reshape(16, 128, 8, 128).transpose(2, 1, 0, 3)),
        "suw_s": np.ascontiguousarray(
            np.asarray(L0_ffn_shared_up_w, np.float32)
            .reshape(16, 128, 8, 128).transpose(2, 1, 0, 3)),
        "sdw": np.ascontiguousarray(np.asarray(L0_ffn_shared_down_w, np.float32)),
        "w13": np.ascontiguousarray(np.asarray(L0_ffn_w13, np.float32)),
        "w2": np.ascontiguousarray(np.asarray(L0_ffn_w2, np.float32)),
    }
    in_maps, own = [], []
    for c in range(NCORE):
        oi = own_index(c)
        own.append(oi)
        xr = np.ascontiguousarray(xflat[oi])
        xtown = np.ascontiguousarray(
            xr.reshape(OWN, 16, 128).transpose(2, 1, 0).reshape(128, 16 * OWN))
        m16 = (np.arange(128)[:, None] <= (16 * c + np.arange(16))[None, :]
               ).astype(np.float32)
        m = dict(shared)
        m["xrows"] = xr
        m["xtown"] = xtown
        m["m16"] = np.ascontiguousarray(m16)
        in_maps.append(m)
    return in_maps, own


def kernel(**inputs):
    in_maps, own = host_layouts(**inputs)
    nc = Build().build()
    res = run_bass_kernel_spmd(nc, in_maps, core_ids=list(range(NCORE)))
    out = np.empty((NT, C), np.float32)
    for c in range(NCORE):
        out[own[c]] = res.results[c]["orows"]
    return out.reshape(B, T, C)


# version 09 (final)
# speedup vs baseline: 1.0905x; 1.0905x over previous
import numpy as np
from contextlib import ExitStack

import concourse.bass as bass
import concourse.mybir as mybir
from concourse.bass_utils import run_bass_kernel_spmd
from concourse.bass import IndirectOffsetOnAxis

F32 = mybir.dt.float32
BF16 = mybir.dt.bfloat16
I32 = mybir.dt.int32
AF = mybir.ActivationFunctionType
ALU = mybir.AluOpType
AX = mybir.AxisListType.X

B, T, C = 2, 4096, 2048
NH, HS, NL = 16, 128, 512
NT = B * T
NCORE = 8
OWN = 1024
WIN = 4
KC = 32
NE, NI, TOPK = 32, 1024, 4
CAPL = 192
LN_EPS = 1e-5
RMS_EPS = 1e-6
ISQ = float(1.0 / np.sqrt(HS))
EXPB = -4.0


def own_index(c: int) -> np.ndarray:
    idx = []
    for b in range(B):
        for i in range(WIN):
            for u in range(7, -1, -1):
                base = b * T + 1024 * i + 128 * u + 16 * c
                idx.extend(range(base, base + 16))
    return np.asarray(idx, dtype=np.int64)


class Ctr:
    def __init__(self, sem):
        self.sem = sem
        self.val = 0

    def inc(self, instr, n=1):
        instr.then_inc(self.sem, n)
        self.val += n
        return self.val

    def dma(self, instr):
        return self.inc(instr, 16)


class Build:
    def __init__(self, taps=(), upto="all"):
        self.taps = set(taps)
        self.upto = upto
        nc = bass.Bass("TRN2", target_bir_lowering=False)
        self.nc = nc
        self.stack = ExitStack()
        d = self.dram_in = {}
        for name, shape in [
            ("xrows", [OWN, C]),
            ("xtown", [128, 16 * OWN]),
            ("xt_tiled", [NT // 128, 128, 16, 128]),
            ("wq_s", [16, 128, 16, 128]),
            ("wqt_s", [16, 128, 16, 128]),
            ("wdkvt_s", [128, 16, NL]),
            ("wuk_s", [128, 16, NL]),
            ("wuv_s", [128, 16, NL]),
            ("wot", [C, C]),
            ("gatet_s", [128, 16, NE]),
            ("normw", [1, C]),
            ("sgw_s", [8, 128, 16, 128]),
            ("suw_s", [8, 128, 16, 128]),
            ("sdw", [NI, C]),
            ("w13", [NE, C, 2 * NI]),
            ("w2", [NE, NI, C]),
            ("m16", [128, 16]),
        ]:
            d[name] = nc.dram_tensor(name, shape, F32, kind="ExternalInput").ap()

        self.orows = nc.dram_tensor("orows", [OWN, C], F32, kind="ExternalOutput").ap()

        self.CKV = nc.dram_tensor("CKV", [NT, NL], BF16)
        self.RSPILL = nc.dram_tensor("RSPILL", [OWN, C], F32)
        self.GBUF = nc.dram_tensor("GBUF", [NE * CAPL, C], BF16)
        self.IDINFO = nc.dram_tensor("IDINFO", [NE * CAPL, 8], I32)
        self.CONTRIB = nc.dram_tensor("CONTRIB", [TOPK * OWN, C], BF16)
        self.SHARED = nc.dram_tensor("SHARED", [OWN, C], BF16)
        self.TAPD = nc.dram_tensor("TAPD", [128, 16 * OWN], BF16)
        self.QTD = nc.dram_tensor("QTD", [128, 16 * OWN], BF16)

        self.tap_out = {}
        for t in self.taps:
            shp, dt = {
                "ckv": ([NT, NL], F32), "keff": ([128, 16 * NL], F32),
                "veff": ([4, 128, C], F32), "qT": ([128, 16 * OWN], F32),
                "attn_r": ([OWN, C], F32), "xnorm": ([OWN, C], F32),
                "logits": ([OWN, NE], F32), "routew": ([OWN, 4], F32),
                "routeid": ([OWN, 4], F32), "pos4": ([OWN, 4], F32),
                "shared": ([OWN, C], F32), "gbuf": ([NE * CAPL, C], F32),
                "idinfo": ([NE * CAPL, 8], I32), "contrib": ([TOPK * OWN, C], F32),
                "dbg": ([128, 3072], BF16),
                "rsp": ([OWN, C], F32),
            }[t]
            self.tap_out[t] = nc.dram_tensor("tap_" + t, shp, dt,
                                             kind="ExternalOutput").ap()

        self.sems = {}
        for s in ["dXT", "dW", "dP", "dT", "tM", "vE", "sE", "gI", "stO",
                  "dW2", "bar", "pA", "pB", "qA", "qB", "rA", "rB",
                  "tA", "tB", "gF", "gS", "wA2", "wB2", "wC2", "wD2", "cW"]:
            self.sems[s] = Ctr(self.stack.enter_context(nc.semaphore(s)))
        self.nbar = 0

    def sb(self, name, shape, dtype, stack=None):
        return (stack or self.stack).enter_context(
            self.nc.sbuf_tensor(name, shape, dtype))

    def psum(self, name, shape, dtype=F32, stack=None):
        return (stack or self.stack).enter_context(
            self.nc.psum_tensor(name, shape, dtype))

    def sem(self, name):
        return self.sems[name]

    def barrier(self):
        block = self.block
        bar = self.sem("bar")
        self.nbar += 1
        tgt = 5 * self.nbar

        for engname in ["tensor", "vector", "scalar", "gpsimd", "sync"]:
            @block.__getattribute__(engname)
            def _(e):
                e.sem_inc(bar.sem, 1)
                e.wait_ge(bar.sem, tgt)
        bar.val = tgt

    def build(self):
        nc = self.nc
        with self.stack:
            with nc.Block() as block:
                self.block = block
                r_live = ExitStack()
                with nc.named_scope("ph0"):
                    self.ph0(r_live)
                with ExitStack() as att_live:
                    with ExitStack() as prep_stack:
                        ps8 = [self.psum(f"pp{i}", [128, NL], stack=prep_stack)
                               for i in range(8)]
                        with nc.named_scope("ph1_ckv"):
                            self.ph1_ckv(ps8, prep_stack)
                        with nc.named_scope("ph2_prep"):
                            self.ph2_prep(ps8, att_live, prep_stack)
                        if self.upto == "prep":
                            self.emit_taps_prep()
                            self.finish()
                            return nc
                        self.barrier()
                    with nc.named_scope("ph4_attn"):
                        self.ph4_attention(att_live)
                    if self.upto == "attnraw":
                        self.emit_taps_attn(self.attn_done_vE)
                        self.finish()
                        return nc
                    with nc.named_scope("ph5_ln1"):
                        self.ph5_ln1()
                    if self.upto == "attn":
                        self.emit_taps_attn(self.ln1_done_vE)
                        self.finish()
                        return nc
                    self.barrier()
                with ExitStack() as ffn_stack:
                    with nc.named_scope("ph6_ffnprep"):
                        self.ph6_ffnprep(ffn_stack)
                    if self.upto == "ffn6":
                        self.emit_taps_ffnprep()
                        self.finish()
                        return nc
                    with nc.named_scope("ph7_shared"):
                        self.ph7_shared(ffn_stack)
                    with nc.named_scope("ph8_spill"):
                        self.ph8_spill_r(ffn_stack)
                    if self.upto == "ffnprep":
                        self.emit_taps_ffnprep()
                        self.finish()
                        return nc
                    self.barrier()
                r_live.close()
                with ExitStack() as moe_stack:
                    with nc.named_scope("ph9_moe"):
                        self.ph9_moe(moe_stack)
                    self.barrier()
                with ExitStack() as ret_stack:
                    with nc.named_scope("ph10_ret"):
                        self.ph10_return(ret_stack)
                if self.taps:
                    self.emit_taps_full()
                self.finish()
        return nc

    def ph0(self, r_live):
        nc, block = self.nc, self.block
        d = self.dram_in
        self.gatet_sb = self.sb("gatet_sb", [128, 16 * NE], BF16)
        self.m16_sb = self.sb("m16_sb", [128, 16], BF16)
        self.consts = self.sb("consts", [128, 8], F32)
        self.lncols = self.sb("lncols", [128, 8], F32)
        self.ones_sb = self.sb("ones_sb", [128, 128], BF16)
        self.iota32 = self.sb("iota32", [128, NE], F32)
        self.iotap = self.sb("iotap", [128, 1], F32)
        self.ident = self.sb("ident", [128, 128], BF16)
        self.identF = self.sb("identF", [128, 128], F32)
        self.cumL = self.sb("cumL", [128, 128], BF16)
        self.wids = self.sb("wids", [128, 8 * 8], F32)
        self.R = self.sb("R", [128, 8 * C], F32, r_live)

        dP, dW, vE = self.sem("dP"), self.sem("dW"), self.sem("vE")

        @block.sync
        def _(sy):
            dP.dma(sy.dma_start(
                out=self.R[:].rearrange("p (tt c) -> p tt c", tt=8),
                in_=d["xrows"].rearrange("(tt p) c -> p tt c", p=128)))
        self.dP_r_loaded = dP.val

        @block.gpsimd
        def _(g):
            dW.dma(g.dma_start(out=self.gatet_sb[:],
                               in_=d["gatet_s"].rearrange("p cc e -> p (cc e)")))
            dW.dma(g.dma_start(out=self.m16_sb[:], in_=d["m16"]))
            g.memset(self.ones_sb[:], 1.0)
            g.iota(self.iota32[:], [[1, NE]], channel_multiplier=0,
                   allow_small_or_imprecise_dtypes=True)
            g.iota(self.iotap[:], [[0, 1]], channel_multiplier=1,
                   allow_small_or_imprecise_dtypes=True)
            gM = self.sem("gF")
            gM.inc(g.memset(self.ident[:], 0.0))
            gM.inc(g.memset(self.identF[:], 0.0))
            gM.inc(g.memset(self.cumL[:], 0.0))
            g.wait_ge(gM.sem, gM.val)
            g.affine_select(self.ident[:], self.ident[:], pattern=[[-1, 128]],
                            compare_op=ALU.not_equal, fill=1.0, base=0,
                            channel_multiplier=1)
            g.affine_select(self.identF[:], self.identF[:], pattern=[[-1, 128]],
                            compare_op=ALU.not_equal, fill=1.0, base=0,
                            channel_multiplier=1)
            ins = g.affine_select(self.cumL[:], self.cumL[:], pattern=[[-1, 128]],
                                  compare_op=ALU.is_ge, fill=1.0, base=0,
                                  channel_multiplier=1)
            vE.inc(ins)
        self.vE_consts = vE.val

        @block.vector
        def _(v):
            v.memset(self.consts[:, 0:1], EXPB)
            vE.inc(v.memset(self.consts[:, 1:2], 0.0))

    def ph1_ckv(self, ps8, stk):
        nc, block = self.nc, self.block
        d = self.dram_in
        with ExitStack() as sub:
            wdkvt_sb = self.sb("wdkvt_sb", [128, 16 * NL], BF16, sub)
            xt_sb = [self.sb(f"xt{i}", [128, C], BF16, sub) for i in range(2)]
            stage = [self.sb(f"ckvst{i}", [128, NL], BF16, sub) for i in range(2)]
            ps_ckv = ps8[0:2]
            tM, vE, dP, dW2 = (self.sem(s) for s in
                               ["tM", "vE", "dP", "dW2"])
            pp = [self.sem("pA"), self.sem("pB")]
            rr = [self.sem("rA"), self.sem("rB")]
            NTC = NT // 128
            tm0, ve0 = tM.val, vE.val
            pp0 = [p.val for p in pp]
            rr0 = [r.val for r in rr]
            dwk = dW2.val

            @block.gpsimd
            def _(g):
                dW2.dma(g.dma_start(
                    out=wdkvt_sb[:],
                    in_=d["wdkvt_s"].rearrange("p cc l -> p (cc l)")))
                for tc in range(NTC):
                    g.wait_ge(tM.sem, max(0, tm0 + tc - 1))
                    pp[tc % 2].dma(g.dma_start(
                        out=xt_sb[tc % 2][:],
                        in_=d["xt_tiled"][tc].rearrange("p cc t -> p (cc t)")))

            @block.tensor
            def _(t):
                t.wait_ge(dW2.sem, dwk + 16)
                for tc in range(NTC):
                    t.wait_ge(pp[tc % 2].sem, pp0[tc % 2] + 16 * (tc // 2 + 1))
                    if tc >= 2:
                        t.wait_ge(vE.sem, ve0 + tc - 1)
                    for cc in range(16):
                        ins = t.matmul(ps_ckv[tc % 2][:],
                                       xt_sb[tc % 2][:, cc * 128:(cc + 1) * 128],
                                       wdkvt_sb[:, cc * NL:(cc + 1) * NL],
                                       start=(cc == 0), stop=(cc == 15))
                    tM.inc(ins)

            @block.vector
            def _(v):
                for tc in range(NTC):
                    v.wait_ge(tM.sem, tm0 + tc + 1)
                    if tc >= 2:
                        v.wait_ge(rr[tc % 2].sem, rr0[tc % 2] + 16 * (tc // 2))
                    vE.inc(v.tensor_copy(stage[tc % 2][:], ps_ckv[tc % 2][:]))

            @block.sync
            def _(sy):
                for tc in range(NTC):
                    sy.wait_ge(vE.sem, ve0 + tc + 1)
                    rr[tc % 2].dma(sy.dma_start(
                        out=self.CKV[tc * 128:(tc + 1) * 128, :],
                        in_=stage[tc % 2][:]))
            self.ckv_done_dP = dP.val
            self.ckv_done_st = (rr[0].val, rr[1].val)
            self.ckv_done_tM = tM.val
            self.ckv_done_vE = vE.val

    def ph2_prep(self, ps8, live, stk):
        nc, block = self.nc, self.block
        d = self.dram_in
        self.keff_sb = self.sb("keff_sb", [128, 16 * NL], BF16, live)
        self.veff_sb = [self.sb(f"veff{lc}", [128, C], BF16, live)
                        for lc in range(4)]
        self.qT_sb = self.sb("qT_sb", [128, 16 * OWN], BF16, stk)
        dW2, tM, vE = (self.sem(s) for s in ["dW2", "tM", "vE"])

        with ExitStack() as sub:
            wuk_sb = self.sb("wuk_sb", [128, 16 * NL], BF16, sub)
            wbuf = [self.sb(f"wbufa{i}", [128, C], BF16, sub) for i in range(2)]
            pp = [self.sem("pA"), self.sem("pB")]
            dw0, tm0, ve0 = dW2.val, tM.val, vE.val
            pp0 = [p.val for p in pp]

            @block.gpsimd
            def _(g):
                g.wait_ge(tM.sem, self.ckv_done_tM)
                g.wait_ge(self.sem("rA").sem, self.ckv_done_st[0])
                g.wait_ge(self.sem("rB").sem, self.ckv_done_st[1])
                dW2.dma(g.dma_start(
                    out=wuk_sb[:],
                    in_=d["wuk_s"].rearrange("p cc l -> p (cc l)")))
                for jc in range(16):
                    g.wait_ge(tM.sem, max(0, tm0 + jc - 1))
                    pp[jc % 2].dma(g.dma_start(
                        out=wbuf[jc % 2][:],
                        in_=d["wq_s"][jc].rearrange("p ic pj -> p (ic pj)")))

            @block.tensor
            def _(t):
                t.wait_ge(vE.sem, self.ckv_done_vE)
                t.wait_ge(dW2.sem, dw0 + 16)
                for jc in range(16):
                    t.wait_ge(pp[jc % 2].sem, pp0[jc % 2] + 16 * (jc // 2 + 1))
                    if jc >= 2:
                        t.wait_ge(vE.sem, ve0 + jc - 1)
                    for ic in range(16):
                        ins = t.matmul(ps8[jc % 2][:],
                                       wbuf[jc % 2][:, ic * 128:(ic + 1) * 128],
                                       wuk_sb[:, ic * NL:(ic + 1) * NL],
                                       start=(ic == 0), stop=(ic == 15))
                    tM.inc(ins)

            @block.vector
            def _(v):
                for jc in range(16):
                    v.wait_ge(tM.sem, tm0 + jc + 1)
                    vE.inc(v.tensor_copy(self.keff_sb[:, jc * NL:(jc + 1) * NL],
                                         ps8[jc % 2][:]))
            keff_tM = tM.val

        with ExitStack() as sub:
            wuv_sb = self.sb("wuv_sb", [128, 16 * NL], BF16, sub)
            wbuf = [self.sb(f"wbufb{i}", [128, C], BF16, sub) for i in range(2)]
            pp = [self.sem("pA"), self.sem("pB")]
            dwv0 = dW2.val

            @block.gpsimd
            def _(g):
                g.wait_ge(tM.sem, keff_tM)
                dW2.dma(g.dma_start(
                    out=wuv_sb[:],
                    in_=d["wuv_s"].rearrange("p cc l -> p (cc l)")))

            for vpass in range(2):
                dwv, tmv, vev = dW2.val, tM.val, vE.val

                ppv = [p.val for p in pp]

                @block.gpsimd
                def _(g, tmv=tmv):
                    for mc in range(16):
                        g.wait_ge(tM.sem, max(0, tmv + mc - 1))
                        pp[mc % 2].dma(g.dma_start(
                            out=wbuf[mc % 2][:],
                            in_=d["wot"][mc * 128:(mc + 1) * 128, :]))

                @block.tensor
                def _(t, vpass=vpass, ppv=ppv, vev=vev):
                    t.wait_ge(dW2.sem, dwv0 + 16)
                    t.wait_ge(vE.sem, vev)
                    for mc in range(16):
                        t.wait_ge(pp[mc % 2].sem, ppv[mc % 2] + 16 * (mc // 2 + 1))
                        ins = None
                        for lc in range(4):
                            for gsub in range(2):
                                grp = vpass * 2 + gsub
                                ins = t.matmul(
                                    ps8[lc * 2 + gsub][:],
                                    wuv_sb[:, mc * NL + lc * 128:
                                           mc * NL + (lc + 1) * 128],
                                    wbuf[mc % 2][:, grp * 512:(grp + 1) * 512],
                                    start=(mc == 0), stop=(mc == 15))
                        tM.inc(ins)

                @block.vector
                def _(v, vpass=vpass, tmv=tmv):
                    v.wait_ge(tM.sem, tmv + 16)
                    ins = None
                    for lc in range(4):
                        for gsub in range(2):
                            grp = vpass * 2 + gsub
                            ins = v.tensor_copy(
                                self.veff_sb[lc][:, grp * 512:(grp + 1) * 512],
                                ps8[lc * 2 + gsub][:])
                    vE.inc(ins)
            veff_tM = tM.val

        with ExitStack() as sub:
            self.xtown_sb = self.sb("xtown_sb", [128, 16 * OWN], BF16, stk)
            wbuf = [self.sb(f"wbufc{i}", [128, C], BF16, sub) for i in range(2)]
            pp = [self.sem("pA"), self.sem("pB")]
            dwq, tmq, veq = dW2.val, tM.val, vE.val
            ppq = [p.val for p in pp]

            @block.gpsimd
            def _(g):
                g.wait_ge(tM.sem, veff_tM)
                dW2.dma(g.dma_start(out=self.xtown_sb[:], in_=d["xtown"]))
                for ic in range(16):
                    g.wait_ge(tM.sem, max(0, tmq + ic - 1))
                    pp[ic % 2].dma(g.dma_start(
                        out=wbuf[ic % 2][:],
                        in_=d["wqt_s"][ic].rearrange("p jc pi -> p (jc pi)")))

            @block.tensor
            def _(t):
                t.wait_ge(vE.sem, veq)
                t.wait_ge(dW2.sem, dwq + 16)
                for ic in range(16):
                    t.wait_ge(pp[ic % 2].sem, ppq[ic % 2] + 16 * (ic // 2 + 1))
                    if ic >= 2:
                        t.wait_ge(vE.sem, veq + 2 * (ic - 1))
                    for jc in range(16):
                        st = wbuf[ic % 2][:, jc * 128:(jc + 1) * 128]
                        t.matmul(ps8[(ic % 2) * 2][:], st,
                                 self.xtown_sb[:, jc * OWN: jc * OWN + 512],
                                 start=(jc == 0), stop=(jc == 15))
                        ins = t.matmul(
                            ps8[(ic % 2) * 2 + 1][:], st,
                            self.xtown_sb[:, jc * OWN + 512:(jc + 1) * OWN],
                            start=(jc == 0), stop=(jc == 15))
                    tM.inc(ins)

            @block.vector
            def _(v):
                for ic in range(16):
                    v.wait_ge(tM.sem, tmq + ic + 1)
                    v.tensor_copy(self.qT_sb[:, ic * OWN: ic * OWN + 512],
                                  ps8[(ic % 2) * 2][:])
                    vE.inc(v.tensor_copy(
                        self.qT_sb[:, ic * OWN + 512:(ic + 1) * OWN],
                        ps8[(ic % 2) * 2 + 1][:]), 2)
            qt_vE = vE.val

            @block.sync
            def _(sy):
                sy.wait_ge(vE.sem, qt_vE)
                self.sem("dP").dma(sy.dma_start(out=self.QTD[:, :],
                                                in_=self.qT_sb[:]))
        self.qtd_done_dP = self.sem("dP").val
        self.prep_done_tM = tM.val
        self.prep_done_vE = vE.val


    def ph4_attention(self, stk):
        nc, block = self.nc, self.block
        for nm in ["aSC", "aYL", "aEX", "aMK"]:
            if nm not in self.sems:
                self.sems[nm] = Ctr(self.stack.enter_context(nc.semaphore(nm)))
        aSC, aYL = self.sem("aSC"), self.sem("aYL")
        dP, dT = self.sem("dP"), self.sem("dT")

        self.ckv_sb = self.sb("ckv_sb", [128, KC * NL], BF16, stk)
        self.ckvT_sb = self.sb("ckvT_sb", [128, 4 * T], BF16, stk)
        self.qTh = self.sb("qTh", [128, 16 * 512], BF16, stk)
        self.QLT = self.sb("QLT", [128, 4 * 512], BF16, stk)
        self.PT = [self.sb(f"PT{i}", [128, 512], BF16, stk) for i in range(2)]
        self.recip_row = self.sb("recip_row", [1, 512], F32, stk)
        self.recipB = self.sb("recipB", [128, 512], F32, stk)
        self.ylnorm = self.sb("ylnorm", [128, 4 * 512], BF16, stk)
        self.yTe = self.sb("yTe", [128, 512], BF16, stk)
        self.ones_f32 = self.sb("ones_f32", [1, 128], F32, stk)
        self.sq_sb = self.sb("sq_sb", [128, C // 2], F32, stk)

        self.dbg_sb = self.sb("dbg_sb", [128, 3072], BF16, stk) \
            if "dbg" in self.tap_out else None
        self.ps_s = [self.psum(f"ps_s{i}", [128, 512], stack=stk) for i in range(2)]
        self.ps_y = [self.psum(f"ps_y{i}", [128, 512], stack=stk) for i in range(4)]
        self.ps_m = self.psum("ps_m", [128, 512], stack=stk)
        self.psb = self.psum("psb", [128, 1024], BF16, stack=stk)

        @block.vector
        def _(v):
            v.wait_ge(self.sem("dW").sem, 32)
            v.memset(self.ones_f32[:], 1.0)

        for b in range(B):
            dp_b = dP.val
            dt_b = dT.val

            @block.sync
            def _(sy, b=b):
                if b > 0:
                    sy.wait_ge(aYL.sem, aYL.val)
                    sy.wait_ge(aSC.sem, aSC.val)
                else:
                    sy.wait_ge(self.sem("rA").sem, self.ckv_done_st[0])
                    sy.wait_ge(self.sem("rB").sem, self.ckv_done_st[1])
                    sy.wait_ge(dP.sem, self.qtd_done_dP)
                dP.dma(sy.dma_start(
                    out=self.ckv_sb[:].rearrange("p (kc l) -> p kc l", kc=KC),
                    in_=self.CKV[b * T:(b + 1) * T, :]
                        .rearrange("(kc p) l -> p kc l", p=128)))
                dP.dma(sy.dma_start(
                    out=self.qTh[:].rearrange("p (ic t) -> p ic t", ic=16),
                    in_=self.QTD[:, :].rearrange("p (ic bt t) -> p ic bt t",
                                                 ic=16, bt=2)[:, :, b]))

            @block.sync
            def _(s, b=b):
                if b > 0:
                    s.wait_ge(aYL.sem, aYL.val)
                    s.wait_ge(aSC.sem, aSC.val)
                else:
                    s.wait_ge(self.sem("rA").sem, self.ckv_done_st[0])
                    s.wait_ge(self.sem("rB").sem, self.ckv_done_st[1])
                for lc in range(4):
                    dT.dma(s.dma_start_transpose(
                        self.ckvT_sb[:, lc * T:(lc + 1) * T],
                        self.CKV[b * T:(b + 1) * T, lc * 128:(lc + 1) * 128]))

            for h in range(NH):
                self._attn_bh(b, h, dp_b, dt_b)

        self.attn_done_vE = self.sem("vE").val

    @staticmethod
    def _regions(j):
        w, up = j // 8, j % 8
        pfx = 16 * (8 - up)
        if up == 0:
            return [(128 * w, 512)]
        out = [(128 * w, 128 * w + pfx)]
        if w < 3:
            out.append((128 * (w + 1), 512))
        return out

    def _attn_bh(self, b, h, dp_b, dt_b):
        nc, block = self.nc, self.block
        aSC, aYL, aEX, aMK = (self.sem(s) for s in ["aSC", "aYL", "aEX", "aMK"])
        vE, sE, dP, dT = (self.sem(s) for s in ["vE", "sE", "dP", "dT"])
        ps_s, ps_y, ps_m, psb = self.ps_s, self.ps_y, self.ps_m, self.psb
        PT, QLT = self.PT, self.QLT
        ex_base, mk_base, yl_base, v_start = aEX.val, aMK.val, aYL.val, vE.val

        veq0 = vE.val

        @block.tensor
        def _(t):
            t.wait_ge(dP.sem, dp_b + 32)
            t.wait_ge(dT.sem, dt_b + 64)
            t.wait_ge(aEX.sem, max(0, ex_base))
            t.wait_ge(vE.sem, v_start)
            for lc in range(4):
                if lc >= 2:
                    t.wait_ge(vE.sem, veq0 + lc - 1)
                aSC.inc(t.matmul(ps_s[lc % 2][:],
                                 self.keff_sb[:, h * NL + lc * 128:
                                              h * NL + (lc + 1) * 128],
                                 self.qTh[:, h * 512:(h + 1) * 512],
                                 start=True, stop=True))
        ql_mm = aSC.val

        @block.vector
        def _(v):
            for lc in range(4):
                v.wait_ge(aSC.sem, ql_mm - 3 + lc)
                vE.inc(v.tensor_copy(QLT[:, lc * 512:(lc + 1) * 512],
                                     ps_s[lc % 2][:]))
        ql_ev = vE.val
        assert ql_ev == veq0 + 4

        nj = KC
        sc_base = aSC.val

        def emit_scores(t, j):
            if j < 2:
                t.wait_ge(vE.sem, ql_ev)
            t.wait_ge(aEX.sem, max(0, ex_base + j - 1))
            ins = None
            for (q0, q1) in self._regions(j):
                for lc in range(4):
                    st = self.ckvT_sb[:, lc * T + j * 128:lc * T + (j + 1) * 128]
                    ins = t.matmul(ps_s[j % 2][:, q0:q1],
                                   st, QLT[:, lc * 512 + q0:lc * 512 + q1],
                                   start=(lc == 0), stop=(lc == 3))
            aSC.inc(ins)

        def emit_ylat(t, j):
            t.wait_ge(aMK.sem, mk_base + j + 1)
            if j == 0:
                t.wait_ge(vE.sem, v_start)
            ins = None
            last = (j == KC - 1)
            for (q0, q1) in self._regions(j):
                for lc in range(4):
                    st = self.ckv_sb[:, j * NL + lc * 128:j * NL + (lc + 1) * 128]
                    ins = t.matmul(ps_y[lc][:, q0:q1], st, PT[j % 2][:, q0:q1],
                                   start=(j == 0), stop=last)
                ins = t.matmul(ps_m[0:1, q0:q1], self.ones_sb[:, 0:1],
                               PT[j % 2][:, q0:q1], start=(j == 0), stop=last)
            aYL.inc(ins)

        @block.tensor
        def _(t):
            emit_scores(t, 0)
            for j in range(nj):
                if j + 1 < nj:
                    emit_scores(t, j + 1)
                emit_ylat(t, j)

        @block.scalar
        def _(s):
            for j in range(nj):
                qb = 128 * (j // 8)
                s.wait_ge(aSC.sem, sc_base + j + 1)
                if j >= 2:
                    s.wait_ge(aYL.sem, yl_base + j - 1)
                aEX.inc(s.activation(PT[j % 2][:, qb:512],
                                     ps_s[j % 2][:, qb:512],
                                     AF.Exp, bias=self.consts[:, 0:1],
                                     scale=ISQ))

        @block.vector
        def _(v):
            for j in range(nj):
                w, up = j // 8, j % 8
                pfx, qb = 16 * (8 - up), 128 * w
                v.wait_ge(aEX.sem, ex_base + j + 1)
                aMK.inc(v.tensor_mul(PT[j % 2][:, qb + pfx - 16:qb + pfx],
                                     PT[j % 2][:, qb + pfx - 16:qb + pfx],
                                     self.m16_sb[:]))

        @block.vector
        def _(v):
            v.wait_ge(aYL.sem, yl_base + nj)
            if b == 0 and h == 0 and "dbg" in self.tap_out:
                v.tensor_copy(self.dbg_sb[0:1, 0:512], ps_m[0:1, 0:512])
                v.tensor_copy(self.dbg_sb[:, 512:1024], ps_y[0][:])
                v.tensor_copy(self.dbg_sb[:, 1024:1536], QLT[:, 0:512])
                v.tensor_copy(self.dbg_sb[:, 1536:2048], PT[(nj - 1) % 2][:])
            vE.inc(v.reciprocal(self.recip_row[:], ps_m[0:1, 0:512]))
        rc_done = vE.val

        @block.tensor
        def _(t):
            t.wait_ge(vE.sem, rc_done)
            aSC.inc(t.matmul(ps_m[:, 0:512], self.ones_f32[:],
                             self.recip_row[:], start=True, stop=True))
        bc_done = aSC.val

        @block.vector
        def _(v):
            v.wait_ge(aSC.sem, bc_done)
            vE.inc(v.tensor_copy(self.recipB[:], ps_m[:, 0:512]))

        rb_done = vE.val

        @block.vector
        def _(v):
            v.wait_ge(vE.sem, rb_done)
            for lc in range(4):
                vE.inc(v.tensor_mul(self.ylnorm[:, lc * 512:(lc + 1) * 512],
                                    ps_y[lc][:], self.recipB[:]))
        div_done = vE.val

        if b == 0 and h == 0 and "dbg" in self.tap_out:
            @block.vector
            def _(v):
                v.tensor_copy(self.dbg_sb[:, 2048:2560], self.ylnorm[:, 0:512])
                vE.inc(v.tensor_copy(self.dbg_sb[:, 2560:3072], self.recipB[:]))

            @block.sync
            def _(sy):
                sy.wait_ge(vE.sem, vE.val)
                self.sem("stO").dma(sy.dma_start(out=self.tap_out["dbg"][:, :],
                                                 in_=self.dbg_sb[:]))

        @block.tensor
        def _(t):
            t.wait_ge(vE.sem, div_done)
            ins = None
            for lc in range(4):
                ins = t.matmul(ps_m[:, 0:512],
                               self.veff_sb[lc][:, h * 128:(h + 1) * 128],
                               self.ylnorm[:, lc * 512:(lc + 1) * 512],
                               start=(lc == 0), stop=(lc == 3))
            aSC.inc(ins)
        pj_done = aSC.val

        @block.vector
        def _(v):
            v.wait_ge(aSC.sem, pj_done)
            vE.inc(v.tensor_copy(self.yTe[:], ps_m[:, 0:512]))
        ye_done = vE.val

        @block.tensor
        def _(t):
            t.wait_ge(vE.sem, ye_done)
            ins = None
            for w in range(WIN):
                ins = t.transpose(psb[:, w * 128:(w + 1) * 128],
                                  self.yTe[:, w * 128:(w + 1) * 128],
                                  self.ident[:])
            aSC.inc(ins)
        tr_done = aSC.val

        @block.vector
        def _(v):
            v.wait_ge(aSC.sem, tr_done)
            for w in range(WIN):
                rt = b * WIN + w
                vE.inc(v.tensor_add(
                    self.R[:, rt * C + h * 128: rt * C + (h + 1) * 128],
                    self.R[:, rt * C + h * 128: rt * C + (h + 1) * 128],
                    psb[:, w * 128:(w + 1) * 128]))

    def ph5_ln1(self):
        self._layernorm_tiles(self.R, list(range(8)), LN_EPS, None)

    def _layernorm_tiles(self, Rt, tiles, eps, out_dma=None):
        nc, block = self.nc, self.block
        vE, sE = self.sem("vE"), self.sem("sE")
        cols = self.lncols
        base_vE = vE.val
        H2 = C // 2

        for k, tt in enumerate(tiles):
            sl = slice(tt * C, (tt + 1) * C)
            slA = slice(tt * C, tt * C + H2)
            slB = slice(tt * C + H2, (tt + 1) * C)
            tile_vE = vE.val

            @block.scalar
            def _(s, slA=slA, tile_vE=tile_vE):
                s.wait_ge(vE.sem, max(base_vE, tile_vE - 4))
                sE.inc(s.activation(self.sq_sb[:], Rt[:, slA], AF.Square))
            sqa_done = sE.val

            @block.vector
            def _(v, sl=sl, sqa_done=sqa_done):
                v.wait_ge(vE.sem, base_vE)
                vE.inc(v.reduce_sum(cols[:, 0:1], Rt[:, sl], axis=AX))
                v.wait_ge(sE.sem, sqa_done)
                vE.inc(v.reduce_sum(cols[:, 1:2], self.sq_sb[:], axis=AX))
            ra_done = vE.val

            @block.scalar
            def _(s, slB=slB, ra_done=ra_done):
                s.wait_ge(vE.sem, ra_done)
                sE.inc(s.activation(self.sq_sb[:], Rt[:, slB], AF.Square))
            sqb_done = sE.val

            @block.vector
            def _(v, sqb_done=sqb_done, ra_done=ra_done):
                v.wait_ge(sE.sem, sqb_done)
                vE.inc(v.reduce_sum(cols[:, 4:5], self.sq_sb[:], axis=AX))
                v.wait_ge(vE.sem, vE.val)
                vE.inc(v.tensor_add(cols[:, 1:2], cols[:, 1:2], cols[:, 4:5]))
                vE.inc(v.tensor_scalar(cols[:, 2:3], cols[:, 0:1], 1.0 / C,
                                       None, ALU.mult))
                v.wait_ge(vE.sem, vE.val)
                vE.inc(v.tensor_scalar(cols[:, 3:4], cols[:, 1:2], 1.0 / C,
                                       None, ALU.mult))
                vE.inc(v.tensor_mul(cols[:, 4:5], cols[:, 2:3], cols[:, 2:3]))
                v.wait_ge(vE.sem, vE.val)
                vE.inc(v.tensor_sub(cols[:, 5:6], cols[:, 3:4], cols[:, 4:5]))
                v.wait_ge(vE.sem, vE.val)
                vE.inc(v.tensor_scalar(cols[:, 5:6], cols[:, 5:6], float(eps),
                                       None, ALU.add))
            var_done = vE.val

            @block.scalar
            def _(s, var_done=var_done):
                s.wait_ge(vE.sem, var_done)
                sE.inc(s.activation(cols[:, 6:7], cols[:, 5:6], AF.Sqrt))
            sd_done = sE.val

            @block.vector
            def _(v, sl=sl, sd_done=sd_done):
                v.wait_ge(sE.sem, sd_done)
                vE.inc(v.reciprocal(cols[:, 7:8], cols[:, 6:7]))
                vE.inc(v.tensor_scalar(cols[:, 2:3], cols[:, 2:3], -1.0, None,
                                       ALU.mult))
                v.wait_ge(vE.sem, vE.val)
                vE.inc(v.tensor_scalar(Rt[:, sl], Rt[:, sl], cols[:, 2:3],
                                       cols[:, 7:8], ALU.add, ALU.mult))
            if out_dma is not None:
                out_dma(tt, vE.val)
        self.ln1_done_vE = vE.val

    def ph6_ffnprep(self, stk):
        nc, block = self.nc, self.block
        d = self.dram_in
        vE, sE, tM, dP, dW2, gI = (self.sem(x) for x in
                                   ["vE", "sE", "tM", "dP", "dW2", "gI"])
        self.XNT = self.sb("XNT", [128, 16 * OWN], BF16, stk)
        self.logits_sb = self.sb("logits_sb", [128, 8 * NE], F32, stk)
        self.rt = self.sb("rt", [128, 24], F32, stk)
        self.rte = self.sb("rte", [128, 4 * NE], F32, stk)
        self.idsu = self.sb("idsu", [128, 8], mybir.dt.uint32, stk)
        self.Mb = self.sb("Mb", [128, NE], BF16, stk)
        self.Mcum = self.sb("Mcum", [128, NE], BF16, stk)
        self.pos_sb = self.sb("pos_sb", [128, NE], F32, stk)
        self.idx_i = self.sb("idx_i", [128, 1], I32, stk)
        self.val_i = self.sb("val_i", [128, 1], I32, stk)

        ln_done = self.ln1_done_vE

        with ExitStack() as psc:
            self.xnorm_sb = self.sb("xnorm_sb", [128, 8 * C], BF16, psc)
            self.xnf = self.sb("xnf", [128, C], F32, psc)
            self.xntf = self.sb("xntf", [128, C], F32, psc)
            self.gatetF = self.sb("gatetF", [128, 16 * NE], F32, psc)
            self.normwB = self.sb("normwB", [128, C], BF16, psc)
            self.sq2 = self.sb("sq2", [128, C // 2], F32, psc)
            self.bigz = self.sb("bigz", [128, 48], I32, psc)
            self.bigz8 = self.sb("bigz8", [128, 48 * 8], I32, psc)
            self.zero2k = self.sb("zero2k", [128, C], BF16, psc)
            normw_row = self.sb("normw_row", [1, C], F32, psc)
            ones_f = self.sb("ones_f", [1, 128], F32, psc)
            ps_n = [self.psum(f"ps_n{i}", [128, 512], stack=psc) for i in range(4)]
            ps_l = self.psum("ps_l", [128, 512], stack=psc)
            ps_c = self.psum("ps_c", [128, 512], stack=psc)
            psb2 = self.psum("psb2", [128, 1024], BF16, stack=psc)

            @block.sync
            def _(sy):
                dP.dma(sy.dma_start(out=normw_row[:], in_=d["normw"]))
                dP.dma(sy.dma_start(
                    out=self.gatetF[:],
                    in_=d["gatet_s"].rearrange("p cc e -> p (cc e)")))
            nw_dP = dP.val

            @block.gpsimd
            def _(g):
                self.bcA = g.to_reg(NE * CAPL - 1)
                self.bcB = g.to_reg(TOPK * OWN - 1)
                gF = self.sem("gF")
                gF.inc(g.memset(self.bigz[:], 100000))
                gF.inc(g.memset(self.bigz8[:], 100000))
                gF.inc(g.memset(self.zero2k[:], 0.0))
                g.wait_ge(gF.sem, gF.val)
                gI.dma(g.dma_start(
                    out=self.IDINFO.ap().rearrange("(p q) o -> p (q o)", p=128),
                    in_=self.bigz8[:]))
                for kk in range(TOPK * OWN // 128):
                    gI.dma(g.dma_start(
                        out=self.CONTRIB[kk * 128:(kk + 1) * 128, :],
                        in_=self.zero2k[:]))
                for kk in range(NE * CAPL // 128):
                    gI.dma(g.dma_start(
                        out=self.GBUF[kk * 128:(kk + 1) * 128, :],
                        in_=self.zero2k[:]))
            self.contrib_zero_gI = gI.val

            @block.vector
            def _(v):
                vE.inc(v.memset(ones_f[:], 1.0))
            onesf_done = vE.val

            @block.tensor
            def _(t):
                t.wait_ge(dP.sem, nw_dP)
                t.wait_ge(vE.sem, onesf_done)
                t.wait_ge(self.sem("dW").sem, 32)
                for q in range(4):
                    tM.inc(t.matmul(ps_n[q][:], ones_f[:],
                                    normw_row[0:1, q * 512:(q + 1) * 512],
                                    start=True, stop=True))
            bc_tM = tM.val

            @block.vector
            def _(v):
                v.wait_ge(tM.sem, bc_tM)
                for q in range(4):
                    vE.inc(v.tensor_copy(self.normwB[:, q * 512:(q + 1) * 512],
                                         ps_n[q][:]))
            nwb_done = vE.val

            self.logit_ev = []
            self.xnf_free = []
            sq_free = 0
            cols = self.lncols
            for tt in range(8):
                sl = slice(tt * C, (tt + 1) * C)
                slA = slice(tt * C, tt * C + C // 2)
                slB = slice(tt * C + C // 2, (tt + 1) * C)

                @block.scalar
                def _(s, slA=slA, sq_free=sq_free):
                    s.wait_ge(vE.sem, max(ln_done, sq_free))
                    sE.inc(s.activation(self.sq2[:], self.R[:, slA], AF.Square))
                sqa = sE.val

                @block.vector
                def _(v, sqa=sqa):
                    v.wait_ge(sE.sem, sqa)
                    vE.inc(v.reduce_sum(cols[:, 0:1], self.sq2[:], axis=AX))
                ra = vE.val

                @block.scalar
                def _(s, slB=slB, ra=ra):
                    s.wait_ge(vE.sem, ra)
                    sE.inc(s.activation(self.sq2[:], self.R[:, slB], AF.Square))
                sqb = sE.val

                @block.vector
                def _(v, sqb=sqb):
                    v.wait_ge(sE.sem, sqb)
                    vE.inc(v.reduce_sum(cols[:, 1:2], self.sq2[:], axis=AX))
                    v.wait_ge(vE.sem, vE.val)
                    vE.inc(v.tensor_add(cols[:, 2:3], cols[:, 0:1], cols[:, 1:2]))
                    v.wait_ge(vE.sem, vE.val)
                    vE.inc(v.tensor_scalar(cols[:, 2:3], cols[:, 2:3], 1.0 / C,
                                           float(RMS_EPS), ALU.mult, ALU.add))
                sq_free = vE.val - 2
                ms_done = vE.val

                @block.scalar
                def _(s, ms_done=ms_done):
                    s.wait_ge(vE.sem, ms_done)
                    sE.inc(s.activation(cols[:, 3:4], cols[:, 2:3], AF.Sqrt))
                sd_done = sE.val

                @block.vector
                def _(v, sl=sl, tt=tt, sd_done=sd_done):
                    v.wait_ge(sE.sem, sd_done)
                    vE.inc(v.reciprocal(cols[:, 4:5], cols[:, 3:4]))
                    v.wait_ge(vE.sem, vE.val)
                    v.wait_ge(vE.sem, nwb_done)
                    if tt >= 1:
                        v.wait_ge(tM.sem, self.xnf_free[tt - 1])
                    vE.inc(v.scalar_tensor_tensor(
                        self.xnf[:], self.R[:, sl], cols[:, 4:5],
                        self.normwB[:], ALU.mult, ALU.mult))
                    v.wait_ge(vE.sem, vE.val)
                    vE.inc(v.tensor_copy(self.xnorm_sb[:, sl], self.xnf[:]))
                xn_done = vE.val

                trp_base = tM.val
                for half in range(2):
                    @block.tensor
                    def _(t, tt=tt, half=half, xn_done=xn_done,
                          trp_base=trp_base):
                        if half == 0:
                            t.wait_ge(vE.sem, xn_done)
                        else:
                            t.wait_ge(vE.sem, xn_done + 8)
                        ins = None
                        for q in range(8):
                            cc = half * 8 + q
                            ins = t.transpose(
                                psb2[:, q * 128:(q + 1) * 128],
                                self.xnorm_sb[:, tt * C + cc * 128:
                                              tt * C + (cc + 1) * 128],
                                self.ident[:])
                        tM.inc(ins)

                    @block.vector
                    def _(v, tt=tt, half=half, trp_base=trp_base):
                        v.wait_ge(tM.sem, trp_base + half + 1)
                        for q in range(8):
                            cc = half * 8 + q
                            vE.inc(v.tensor_copy(
                                self.XNT[:, cc * OWN + tt * 128:
                                         cc * OWN + (tt + 1) * 128],
                                psb2[:, q * 128:(q + 1) * 128]))
                xnt_done = vE.val

                trf_base = tM.val
                vef_base = vE.val

                @block.tensor
                def _(t, tt=tt, xn_done=xn_done, trf_base=trf_base,
                      vef_base=vef_base):
                    t.wait_ge(vE.sem, xn_done)
                    for cc in range(16):
                        if cc >= 4:
                            t.wait_ge(vE.sem, vef_base + cc - 3)
                        tM.inc(t.transpose(
                            ps_n[cc % 4][:, 0:128],
                            self.xnf[:, cc * 128:(cc + 1) * 128],
                            self.identF[:]))

                @block.vector
                def _(v, tt=tt, trf_base=trf_base):
                    for cc in range(16):
                        v.wait_ge(tM.sem, trf_base + cc + 1)
                        vE.inc(v.tensor_copy(
                            self.xntf[:, cc * 128:(cc + 1) * 128],
                            ps_n[cc % 4][:, 0:128]))
                xntf_done = vE.val

                @block.tensor
                def _(t, tt=tt, xntf_done=xntf_done):
                    t.wait_ge(vE.sem, xntf_done)
                    if tt >= 1:
                        t.wait_ge(vE.sem, self.logit_ev[tt - 1])
                    ins = None
                    for cc in range(16):
                        ins = t.matmul(ps_l[:, 0:NE],
                                       self.xntf[:, cc * 128:(cc + 1) * 128],
                                       self.gatetF[:, cc * NE:(cc + 1) * NE],
                                       start=(cc == 0), stop=(cc == 15))
                    tM.inc(ins)
                self.xnf_free.append(tM.val)
                lg_mm = tM.val

                @block.vector
                def _(v, tt=tt, lg_mm=lg_mm):
                    v.wait_ge(tM.sem, lg_mm)
                    vE.inc(v.tensor_copy(
                        self.logits_sb[:, tt * NE:(tt + 1) * NE], ps_l[:, 0:NE]))
                self.logit_ev.append(vE.val)

            for tt in range(8):
                self._route_tile(tt, ps_c, gI)
            self.route_done_vE = vE.val
            self.route_done_gI = gI.val

            if "xnorm" in self.tap_out:
                stO = self.sem("stO")

                @block.sync
                def _(sy):
                    sy.wait_ge(vE.sem, vE.val)
                    stO.dma(sy.dma_start(out=self.TAPD[:, :8 * C],
                                         in_=self.xnorm_sb[:]))

                @block.gpsimd
                def _(g):
                    g.wait_ge(stO.sem, stO.val)
                    stO.dma(g.dma_start(
                        out=self.tap_out["xnorm"].rearrange(
                            "(tt p) c -> p tt c", p=128),
                        in_=self.TAPD[:, :8 * C].rearrange(
                            "p (tt c) -> p tt c", tt=8)))
        self.xn_ready_vE = vE.val

    def _route_tile(self, tt, ps_c, gI):
        nc, block = self.nc, self.block
        vE, sE, tM = (self.sem(x) for x in ["vE", "sE", "tM"])
        cols = self.rt
        lg = self.logits_sb[:, tt * NE:(tt + 1) * NE]
        gI_base = gI.val

        @block.vector
        def _(v):
            v.wait_ge(vE.sem, self.logit_ev[tt])
            vE.inc(v.max(out=cols[:, 0:8], in_=lg))
            v.wait_ge(vE.sem, vE.val)
            vE.inc(v.max_index(out=self.idsu[:], in_max=cols[:, 0:8],
                               in_values=lg))
            v.wait_ge(vE.sem, vE.val)
            vE.inc(v.tensor_copy(cols[:, 12:16], self.idsu[:, 0:4]))
            vE.inc(v.tensor_scalar(cols[:, 16:20], cols[:, 0:4], cols[:, 0:1],
                                   None, ALU.subtract))
        sm_done = vE.val

        @block.scalar
        def _(s):
            s.wait_ge(vE.sem, sm_done)
            sE.inc(s.activation(cols[:, 16:20], cols[:, 16:20], AF.Exp))
        ex_done = sE.val

        @block.vector
        def _(v):
            v.wait_ge(sE.sem, ex_done)
            vE.inc(v.reduce_sum(cols[:, 20:21], cols[:, 16:20], axis=AX))
            v.wait_ge(vE.sem, vE.val)
            vE.inc(v.reciprocal(cols[:, 21:22], cols[:, 20:21]))
            v.wait_ge(vE.sem, vE.val)
            vE.inc(v.tensor_scalar(self.wids[:, tt * 8 + 4:tt * 8 + 8],
                                   cols[:, 16:20], cols[:, 21:22], None,
                                   ALU.mult))
            vE.inc(v.tensor_copy(self.wids[:, tt * 8:tt * 8 + 4],
                                 cols[:, 12:16]))
            v.wait_ge(vE.sem, vE.val)
            for k in range(4):
                vE.inc(v.tensor_tensor(
                    self.rte[:, k * NE:(k + 1) * NE],
                    cols[:, 12 + k:13 + k].to_broadcast([128, NE]),
                    self.iota32[:], ALU.is_equal))
            v.wait_ge(vE.sem, vE.val)
            vE.inc(v.tensor_add(self.pos_sb[:], self.rte[:, 0:NE],
                                self.rte[:, NE:2 * NE]))
            vE.inc(v.tensor_add(self.Mb[:], self.rte[:, 2 * NE:3 * NE],
                                self.rte[:, 3 * NE:4 * NE]))
            v.wait_ge(vE.sem, vE.val)
            vE.inc(v.tensor_add(self.Mb[:], self.Mb[:], self.pos_sb[:]))
        m_done = vE.val

        @block.tensor
        def _(t):
            t.wait_ge(vE.sem, m_done)
            if tt == 0:
                ins = t.matmul(ps_c[:, 0:NE], self.cumL[:], self.Mb[:],
                               start=True, stop=True)
            else:
                t.matmul(ps_c[:, 0:NE], self.cumL[:], self.Mb[:],
                         start=True, stop=False)
                ins = t.matmul(ps_c[:, 0:NE], self.ones_sb[:], self.Mcum[:],
                               start=False, stop=True)
            tM.inc(ins)
        cs_done = tM.val

        @block.vector
        def _(v):
            v.wait_ge(tM.sem, cs_done)
            vE.inc(v.tensor_copy(self.pos_sb[:], ps_c[:, 0:NE]))
            if tt == 0:
                vE.inc(v.tensor_copy(self.Mcum[:], self.Mb[:]))
            else:
                vE.inc(v.tensor_add(self.Mcum[:], self.Mcum[:], self.Mb[:]))
        pos_done = vE.val

        for k in range(4):
            @block.vector
            def _(v, k=k):
                v.wait_ge(vE.sem, pos_done)
                v.wait_ge(gI.sem, gI_base + 32 * k if (tt or k) else 0)
                vE.inc(v.tensor_mul(self.rte[:, k * NE:(k + 1) * NE],
                                    self.rte[:, k * NE:(k + 1) * NE],
                                    self.pos_sb[:]))
                v.wait_ge(vE.sem, vE.val)
                vE.inc(v.reduce_sum(cols[:, 22:23],
                                    self.rte[:, k * NE:(k + 1) * NE], axis=AX))
                v.wait_ge(vE.sem, vE.val)
                vE.inc(v.scalar_tensor_tensor(cols[:, 23:24],
                                              cols[:, 12 + k:13 + k],
                                              float(CAPL), cols[:, 22:23],
                                              ALU.mult, ALU.add))
                vE.inc(v.tensor_scalar(cols[:, 20:21], cols[:, 22:23],
                                       float(CAPL), None, ALU.is_ge))
                v.wait_ge(vE.sem, vE.val)
                vE.inc(v.scalar_tensor_tensor(cols[:, 23:24], cols[:, 20:21],
                                              1.0e5, cols[:, 23:24],
                                              ALU.mult, ALU.add))
                vE.inc(v.tensor_scalar(cols[:, 22:23], self.iotap[:],
                                       float(k * OWN + tt * 128), None,
                                       ALU.add))
                v.wait_ge(vE.sem, vE.val)
                vE.inc(v.tensor_copy(self.idx_i[:], cols[:, 23:24]))
                vE.inc(v.tensor_copy(self.val_i[:], cols[:, 22:23]))
            sc_ready = vE.val

            @block.gpsimd
            def _(g, k=k, tt=tt, sc_ready=sc_ready):
                g.wait_ge(vE.sem, sc_ready)
                if tt == 0 and k == 0:
                    g.wait_ge(gI.sem, self.contrib_zero_gI)
                gI.dma(g.indirect_dma_start(
                    out=self.GBUF.ap(),
                    out_offset=IndirectOffsetOnAxis(self.idx_i[:, 0:1], 0),
                    in_=self.xnorm_sb[:, tt * C:(tt + 1) * C], in_offset=None,
                    bounds_check=self.bcA, oob_is_err=False))
                gI.dma(g.indirect_dma_start(
                    out=self.IDINFO.ap(),
                    out_offset=IndirectOffsetOnAxis(self.idx_i[:, 0:1], 0),
                    in_=self.val_i[:], in_offset=None,
                    bounds_check=self.bcA, oob_is_err=False))

    def ph7_shared(self, stk):
        nc, block = self.nc, self.block
        d = self.dram_in
        vE, sE, tM, dP, dW2 = (self.sem(x) for x in
                               ["vE", "sE", "tM", "dP", "dW2"])
        with ExitStack() as sub:
            sdw_sb = self.sb("sdw_sb", [128, 8 * C], BF16, sub)
            act_sb = self.sb("act_sb", [128, 8 * OWN], BF16, sub)
            wg = [self.sb(f"wg{i}", [128, C], BF16, sub) for i in range(2)]
            wu = [self.sb(f"wu{i}", [128, C], BF16, sub) for i in range(2)]
            silu_sb = self.sb("silu_sb", [128, 512], BF16, sub)
            shst = [self.sb(f"shst{i}", [128, C], BF16, sub) for i in range(2)]
            ps_g = [self.psum(f"ps_g{i}", [128, 512], stack=sub) for i in range(2)]
            ps_u = [self.psum(f"ps_u{i}", [128, 512], stack=sub) for i in range(2)]
            ps_d = self.psum("ps_d", [128, 2048], stack=sub)

            dw0 = dW2.val
            tm0 = tM.val
            xn_need = self.xn_ready_vE
            pp = [self.sem("pA"), self.sem("pB")]
            qq = [self.sem("qA"), self.sem("qB")]
            rr = [self.sem("rA"), self.sem("rB")]
            pp0 = [p.val for p in pp]
            qq0 = [q.val for q in qq]
            rr0 = [r.val for r in rr]

            @block.gpsimd
            def _(g):
                g.wait_ge(self.sem("gI").sem, self.route_done_gI)
                for ic in range(8):
                    dW2.dma(g.dma_start(out=sdw_sb[:, ic * C:(ic + 1) * C],
                                        in_=d["sdw"][ic * 128:(ic + 1) * 128, :]))
                for ic in range(8):
                    g.wait_ge(tM.sem, max(0, tm0 + ic - 1))
                    pp[ic % 2].dma(g.dma_start(
                        out=wg[ic % 2][:],
                        in_=d["sgw_s"][ic].rearrange("p cc pi -> p (cc pi)")))
                    qq[ic % 2].dma(g.dma_start(
                        out=wu[ic % 2][:],
                        in_=d["suw_s"][ic].rearrange("p cc pi -> p (cc pi)")))

            for ic in range(8):
                vec = vE.val

                @block.tensor
                def _(t, ic=ic, vec=vec):
                    t.wait_ge(pp[ic % 2].sem, pp0[ic % 2] + 16 * (ic // 2 + 1))
                    t.wait_ge(qq[ic % 2].sem, qq0[ic % 2] + 16 * (ic // 2 + 1))
                    t.wait_ge(vE.sem, xn_need)
                    if ic >= 1:
                        t.wait_ge(vE.sem, vec)
                    ins = None
                    for half in range(2):
                        for cc in range(16):
                            t.matmul(ps_g[half][:],
                                     wg[ic % 2][:, cc * 128:(cc + 1) * 128],
                                     self.XNT[:, cc * OWN + half * 512:
                                              cc * OWN + (half + 1) * 512],
                                     start=(cc == 0), stop=(cc == 15))
                            ins = t.matmul(
                                ps_u[half][:],
                                wu[ic % 2][:, cc * 128:(cc + 1) * 128],
                                self.XNT[:, cc * OWN + half * 512:
                                         cc * OWN + (half + 1) * 512],
                                start=(cc == 0), stop=(cc == 15))
                    tM.inc(ins)
                gu_done = tM.val

                se_b = sE.val

                @block.scalar
                def _(s, gu_done=gu_done, vec=vec):
                    for half in range(2):
                        s.wait_ge(tM.sem, gu_done)
                        if half == 1:
                            s.wait_ge(vE.sem, vec + 1)
                        sE.inc(s.activation(silu_sb[:], ps_g[half][:], AF.Silu))

                @block.vector
                def _(v, ic=ic, se_b=se_b):
                    for half in range(2):
                        v.wait_ge(sE.sem, se_b + half + 1)
                        vE.inc(v.tensor_mul(
                            act_sb[:, ic * OWN + half * 512:
                                   ic * OWN + (half + 1) * 512],
                            silu_sb[:], ps_u[half][:]))
            act_done_vE = vE.val

            ev_list = []
            dn0 = dP.val
            for tt in range(8):
                @block.tensor
                def _(t, tt=tt):
                    t.wait_ge(vE.sem, act_done_vE)
                    t.wait_ge(dW2.sem, dw0 + 16 * 8)
                    if tt >= 1:
                        t.wait_ge(vE.sem, ev_list[tt - 1])
                    ins = None
                    for ic in range(8):
                        st = act_sb[:, ic * OWN + tt * 128:
                                    ic * OWN + (tt + 1) * 128]
                        for q in range(4):
                            ins = t.matmul(ps_d[:, q * 512:(q + 1) * 512], st,
                                           sdw_sb[:, ic * C + q * 512:
                                                  ic * C + (q + 1) * 512],
                                           start=(ic == 0), stop=(ic == 7))
                    tM.inc(ins)
                dn_done = tM.val

                @block.vector
                def _(v, tt=tt, dn_done=dn_done):
                    v.wait_ge(tM.sem, dn_done)
                    if tt >= 2:
                        v.wait_ge(rr[tt % 2].sem, rr0[tt % 2] + 16 * (tt // 2))
                    vE.inc(v.tensor_copy(shst[tt % 2][:], ps_d[:]))
                ev_list.append(vE.val)

                @block.sync
                def _(sy, tt=tt, ev=ev_list[tt]):
                    sy.wait_ge(vE.sem, ev)
                    rr[tt % 2].dma(sy.dma_start(
                        out=self.SHARED[tt * 128:(tt + 1) * 128, :],
                        in_=shst[tt % 2][:]))
            self.shared_done_st = (rr[0].val, rr[1].val)
            self.shared_done_dP = dP.val

    def ph8_spill_r(self, stk):
        nc, block = self.nc, self.block
        dP, vE = self.sem("dP"), self.sem("vE")

        @block.sync
        def _(sy):
            sy.wait_ge(vE.sem, self.xn_ready_vE)
            dP.dma(sy.dma_start(
                out=self.RSPILL.ap().rearrange("(tt p) c -> p tt c", p=128),
                in_=self.R[:].rearrange("p (tt c) -> p tt c", tt=8)))
        self.rspill_done_dP = dP.val

    def ph9_moe(self, stk):
        nc, block = self.nc, self.block
        d = self.dram_in
        vE, sE, tM, dW2, dT, gI = (self.sem(x) for x in
                                   ["vE", "sE", "tM", "dW2", "dT", "gI"])
        GR = 4
        GW = GR * CAPL
        w13buf = [self.sb(f"w13b{i}", [128, 16 * NI], BF16, stk)
                  for i in range(2)]
        w2buf = [self.sb(f"w2b{i}", [128, 8 * C], BF16, stk) for i in range(2)]
        gbufT = [self.sb(f"gbT{i}", [128, 16 * GW], BF16, stk)
                 for i in range(2)]
        g_sb = self.sb("g_sb", [128, 8 * CAPL], BF16, stk)
        act_sb = self.sb("act_sb2", [128, 8 * CAPL], BF16, stk)
        silu2 = self.sb("silu2", [128, CAPL], BF16, stk)
        outst = [self.sb(f"outst{i}", [128, C], BF16, stk) for i in range(2)]
        w2stg = [self.sb(f"w2stg{i}", [128, 512], F32, stk) for i in range(4)]
        idxa = [self.sb(f"idxa{i}", [128, 1], I32, stk) for i in range(2)]
        idxb = [self.sb(f"idxb{i}", [64, 1], I32, stk) for i in range(2)]
        ps_w = [self.psum(f"ps_w{i}", [128, 512], stack=stk) for i in range(3)]
        ps_o2 = self.psum("ps_o2", [128, 2048], stack=stk)

        u_done = [0] * NE
        g_done = [0] * NE
        w2_done = [0] * NE
        mu_done = [0] * NE
        w2ev = [0] * NE
        sc_done = [0] * NE
        self.moe_pp = pp = [self.sem("pA"), self.sem("pB")]
        wW = [self.sem("wA2"), self.sem("wB2"), self.sem("wC2"), self.sem("wD2")]
        wW0 = [w.val for w in wW]
        cW = self.sem("cW")
        cb0 = cW.val
        self.moe_qq = qq = [self.sem("qA"), self.sem("qB")]
        self.moe_tt = tt_s = [self.sem("tA"), self.sem("tB")]
        self.moe_pp0 = pp0 = [p.val for p in pp]
        self.moe_qq0 = qq0 = [q.val for q in qq]
        self.moe_tt0 = tt0 = [t.val for t in tt_s]
        self.moe_dT0 = dT.val

        for e in range(NE):
            grp, eg = e // GR, e % GR

            if eg == 0:
                @block.sync
                def _(sc, grp=grp):
                    if grp == 0:
                        sc.wait_ge(gI.sem, self.route_done_gI)
                        sc.wait_ge(self.sem("dP").sem, self.rspill_done_dP)
                        sc.wait_ge(self.sem("rA").sem, self.shared_done_st[0])
                        sc.wait_ge(self.sem("rB").sem, self.shared_done_st[1])
                    if grp >= 2:
                        sc.wait_ge(tM.sem, u_done[grp * GR - GR - 1])
                    for cc in range(16):
                        tt_s[grp % 2].dma(sc.dma_start_transpose(
                            gbufT[grp % 2][:, cc * GW:(cc + 1) * GW],
                            self.GBUF[grp * GW:(grp + 1) * GW,
                                      cc * 128:(cc + 1) * 128]))
            dT_need = (grp % 2, tt0[grp % 2] + 256 * (grp // 2 + 1))

            dwb = dW2.val

            @block.gpsimd
            def _(g, e=e):
                if e == 0:
                    g.wait_ge(gI.sem, self.contrib_zero_gI)
                    g.wait_ge(self.sem("dP").sem, self.rspill_done_dP)
                    g.wait_ge(self.sem("rA").sem, self.shared_done_st[0])
                    g.wait_ge(self.sem("rB").sem, self.shared_done_st[1])
                if e >= 1:
                    g.wait_ge(tM.sem, g_done[e - 1])
                for cc in range(16):
                    pp[0].dma(g.dma_start(
                        out=w13buf[0][:, cc * NI:(cc + 1) * NI],
                        in_=d["w13"][e, cc * 128:(cc + 1) * 128, 0:NI]))
                if e >= 1:
                    g.wait_ge(tM.sem, u_done[e - 1])
                for cc in range(16):
                    pp[1].dma(g.dma_start(
                        out=w13buf[1][:, cc * NI:(cc + 1) * NI],
                        in_=d["w13"][e, cc * 128:(cc + 1) * 128, NI:2 * NI]))
                if e >= 2:
                    g.wait_ge(tM.sem, w2_done[e - 2])
                    g.wait_ge(self.sem("gS").sem, sc_done[e - 2])
                for ic in range(8):
                    qq[e % 2].dma(g.dma_start(
                        out=w2buf[e % 2][:, ic * C:ic * C + 1024],
                        in_=d["w2"][e, ic * 128:(ic + 1) * 128, 0:1024]))
                gI.dma(g.dma_start(
                    out=idxa[e % 2][:],
                    in_=self.IDINFO[e * CAPL:e * CAPL + 128, 0:1],
                    allow_slow_non_contiguous=True))
                gI.dma(g.dma_start(
                    out=idxb[e % 2][:],
                    in_=self.IDINFO[e * CAPL + 128:(e + 1) * CAPL, 0:1],
                    allow_slow_non_contiguous=True))
            dw_h0 = pp0[0] + 16 * 16 * (e + 1)
            dw_h1 = pp0[1] + 16 * 16 * (e + 1)
            idx_gI = gI.val

            @block.sync
            def _(sy, e=e):
                for k in range(16):
                    m = e * 16 + k
                    if m >= 4:
                        sy.wait_ge(cW.sem, cb0 + m - 3)
                    wW[m % 4].dma(sy.dma_start(
                        out=w2stg[m % 4][:],
                        in_=d["w2"][e, (k // 2) * 128:(k // 2 + 1) * 128,
                                    1024 + (k % 2) * 512:
                                    1024 + (k % 2 + 1) * 512]))

            @block.scalar
            def _(sc, e=e):
                if e >= 2:
                    sc.wait_ge(tM.sem, w2_done[e - 2])
                    sc.wait_ge(self.sem("gS").sem, sc_done[e - 2])
                for k in range(16):
                    m = e * 16 + k
                    sc.wait_ge(wW[m % 4].sem, wW0[m % 4] + 16 * (m // 4 + 1))
                    cW.inc(sc.activation(
                        w2buf[e % 2][:, (k // 2) * C + 1024 + (k % 2) * 512:
                                      (k // 2) * C + 1024 + (k % 2 + 1) * 512],
                        w2stg[m % 4][:], AF.Copy))

            vg_base = vE.val

            @block.tensor
            def _(t, e=e, grp=grp, eg=eg, dw_h0=dw_h0, dT_need=dT_need,
                  vg_base=vg_base):
                t.wait_ge(pp[0].sem, dw_h0)
                t.wait_ge(tt_s[dT_need[0]].sem, dT_need[1])
                if e >= 1:
                    t.wait_ge(vE.sem, mu_done[e - 1])
                for fc in range(8):
                    if fc >= 3:
                        t.wait_ge(vE.sem, vg_base + fc - 2)
                    ins = None
                    for cc in range(16):
                        ins = t.matmul(
                            ps_w[fc % 3][:, 0:CAPL],
                            w13buf[0][:, cc * NI + fc * 128:
                                      cc * NI + (fc + 1) * 128],
                            gbufT[grp % 2][:, cc * GW + eg * CAPL:
                                           cc * GW + (eg + 1) * CAPL],
                            start=(cc == 0), stop=(cc == 15))
                    tM.inc(ins)
            g_done[e] = tM.val
            gm_base = g_done[e] - 8

            @block.vector
            def _(v, gm_base=gm_base):
                for fc in range(8):
                    v.wait_ge(tM.sem, gm_base + fc + 1)
                    vE.inc(v.tensor_copy(g_sb[:, fc * CAPL:(fc + 1) * CAPL],
                                         ps_w[fc % 3][:, 0:CAPL]))
            assert vE.val == vg_base + 8

            vu_base = vE.val

            @block.tensor
            def _(t, e=e, grp=grp, eg=eg, dw_h1=dw_h1, vu_base=vu_base,
                  vg_base=vg_base):
                t.wait_ge(pp[1].sem, dw_h1)
                t.wait_ge(vE.sem, vg_base + 8)
                for fc in range(8):
                    if fc >= 3:
                        t.wait_ge(vE.sem, vu_base + fc - 2)
                    ins = None
                    for cc in range(16):
                        ins = t.matmul(
                            ps_w[fc % 3][:, 0:CAPL],
                            w13buf[1][:, cc * NI + fc * 128:
                                      cc * NI + (fc + 1) * 128],
                            gbufT[grp % 2][:, cc * GW + eg * CAPL:
                                           cc * GW + (eg + 1) * CAPL],
                            start=(cc == 0), stop=(cc == 15))
                    tM.inc(ins)
            u_done[e] = tM.val
            um_base = u_done[e] - 8

            sE_base = sE.val

            @block.scalar
            def _(sc, um_base=um_base, vu_base=vu_base):
                for fc in range(8):
                    sc.wait_ge(tM.sem, um_base + fc + 1)
                    if fc >= 1:
                        sc.wait_ge(vE.sem, vu_base + fc)
                    sE.inc(sc.activation(silu2[:, 0:CAPL],
                                         g_sb[:, fc * CAPL:(fc + 1) * CAPL],
                                         AF.Silu))

            @block.vector
            def _(v, sE_base=sE_base):
                for fc in range(8):
                    v.wait_ge(sE.sem, sE_base + fc + 1)
                    vE.inc(v.tensor_mul(act_sb[:, fc * CAPL:(fc + 1) * CAPL],
                                        silu2[:, 0:CAPL],
                                        ps_w[fc % 3][:, 0:CAPL]))
            mu_done[e] = vE.val
            assert mu_done[e] == vu_base + 8

            w2ev_base = vE.val

            @block.tensor
            def _(t, e=e, w2ev_base=w2ev_base):
                t.wait_ge(cW.sem, cb0 + 16 * (e + 1))
                t.wait_ge(qq[e % 2].sem, qq0[e % 2] + 16 * 8 * (e // 2 + 1))
                t.wait_ge(vE.sem, mu_done[e])
                if e >= 1:
                    t.wait_ge(vE.sem, w2ev[e - 1])
                ins = None
                for sc in range(2):
                    w = 128 if sc == 0 else CAPL - 128
                    if sc == 1:
                        t.wait_ge(vE.sem, w2ev_base + 1)
                    for ic in range(8):
                        st = act_sb[:, ic * CAPL + sc * 128:
                                    ic * CAPL + sc * 128 + w]
                        for q in range(4):
                            ins = t.matmul(
                                ps_o2[0:w, q * 512:(q + 1) * 512], st,
                                w2buf[e % 2][:, ic * C + q * 512:
                                             ic * C + (q + 1) * 512],
                                start=(ic == 0), stop=(ic == 7))
                    tM.inc(ins)
            w2_done[e] = tM.val

            for sc in range(2):
                w = 128 if sc == 0 else CAPL - 128

                @block.vector
                def _(v, e=e, sc=sc, w=w):
                    v.wait_ge(tM.sem, w2_done[e] - 1 + sc)
                    if e >= 1:
                        v.wait_ge(self.sem("gS").sem, sc_done[e - 1])
                    vE.inc(v.tensor_copy(outst[sc][0:w, :], ps_o2[0:w, :]))
            w2ev[e] = vE.val

            gS = self.sem("gS")

            @block.gpsimd
            def _(g, e=e, idx_gI=idx_gI, w2ev_e=w2ev[e]):
                g.wait_ge(gI.sem, idx_gI)
                g.wait_ge(vE.sem, w2ev_e - 1)
                gS.dma(g.indirect_dma_start(
                    out=self.CONTRIB.ap(),
                    out_offset=IndirectOffsetOnAxis(idxa[e % 2][:, 0:1], 0),
                    in_=outst[0][0:128, :], in_offset=None,
                    bounds_check=self.bcB, oob_is_err=False))
                g.wait_ge(vE.sem, w2ev_e)
                gS.dma(g.indirect_dma_start(
                    out=self.CONTRIB.ap(),
                    out_offset=IndirectOffsetOnAxis(idxb[e % 2][:, 0:1], 0),
                    in_=outst[1][0:CAPL - 128, :], in_offset=None,
                    bounds_check=self.bcB, oob_is_err=False))
            sc_done[e] = gS.val
        self.moe_done_gI = gS.val

    def ph10_return(self, stk):
        nc, block = self.nc, self.block
        vE, sE, dP, gI, stO = (self.sem(x) for x in
                               ["vE", "sE", "dP", "gI", "stO"])
        acc = [self.sb(f"acc{i}", [128, C], F32, stk) for i in range(2)]
        rl = self.sb("rl", [128, C], F32, stk)
        sh = self.sb("sh", [128, C], BF16, stk)
        cb = [self.sb(f"cb{i}", [128, C], BF16, stk) for i in range(4)]
        self.sq_sb = self.sb("sq3", [128, C // 2], F32, stk)
        self.ret_done = []

        for tt in range(8):
            ld0 = dP.val

            @block.sync
            def _(sy, tt=tt):
                sy.wait_ge(self.sem("gS").sem, self.moe_done_gI)
                sy.wait_ge(dP.sem, self.rspill_done_dP)
                sy.wait_ge(self.sem("rA").sem, self.shared_done_st[0])
                sy.wait_ge(self.sem("rB").sem, self.shared_done_st[1])
                if tt >= 1:
                    sy.wait_ge(vE.sem, self.ret_done[tt - 1])
                dP.dma(sy.dma_start(out=rl[:],
                                    in_=self.RSPILL[tt * 128:(tt + 1) * 128, :]))
                dP.dma(sy.dma_start(out=sh[:],
                                    in_=self.SHARED[tt * 128:(tt + 1) * 128, :]))
                for k in range(4):
                    dP.dma(sy.dma_start(
                        out=cb[k][:],
                        in_=self.CONTRIB[k * OWN + tt * 128:
                                         k * OWN + (tt + 1) * 128, :]))
            ld_done = dP.val

            @block.vector
            def _(v, tt=tt, ld_done=ld_done, st_prev=stO.val):
                v.wait_ge(dP.sem, ld_done)
                v.wait_ge(stO.sem, st_prev)
                vE.inc(v.scalar_tensor_tensor(acc[0][:], rl[:], 2.0, sh[:],
                                              ALU.mult, ALU.add))
                cur = 0
                for k in range(4):
                    v.wait_ge(vE.sem, vE.val)
                    vE.inc(v.scalar_tensor_tensor(
                        acc[1 - cur][:], cb[k][:],
                        self.wids[:, tt * 8 + 4 + k:tt * 8 + 5 + k],
                        acc[cur][:], ALU.mult, ALU.add))
                    cur = 1 - cur
                self.ret_cur = cur

            cur = self.ret_cur

            def store(dummy_tt, ln_vE, tt=tt, cur=cur):
                @block.sync
                def _(sy):
                    sy.wait_ge(vE.sem, ln_vE)
                    stO.dma(sy.dma_start(
                        out=self.orows[tt * 128:(tt + 1) * 128, :],
                        in_=acc[cur][:]))

            self._layernorm_tiles(acc[cur], [0], LN_EPS, out_dma=store)
            self.ret_done.append(vE.val)

    def emit_taps_prep(self):
        self.tap_bounce([("qT", self.qT_sb, 16 * OWN), ("keff", self.keff_sb, 16 * NL)]
                        + [(f"veff{lc}", self.veff_sb[lc], C) for lc in range(4)])

    def tap_bounce(self, items, pre_dP=None, pre_vE=None):
        nc, block = self.nc, self.block
        stO, vE, dP = (self.sem(s) for s in ["stO", "vE", "dP"])
        tap = self.tap_out

        for name, src, width in items:
            base = name.rstrip("0123456789")
            if base not in tap and name not in tap:
                continue
            tname = name if name in tap else base
            dst = tap[tname]
            if base != name:
                dst = dst[int(name[len(base):])]

            @block.sync
            def _(sy, src=src, width=width):
                sy.wait_ge(vE.sem, vE.val)
                sy.wait_ge(self.sem("tM").sem, self.sem("tM").val)
                sy.wait_ge(stO.sem, stO.val)
                dP.dma(sy.dma_start(out=self.TAPD[:, :width], in_=src[:]))

            @block.gpsimd
            def _(g, dst=dst, width=width):
                g.wait_ge(dP.sem, dP.val)
                stO.dma(g.dma_start(out=dst[:, :] if len(dst.shape) == 2 else dst,
                                    in_=self.TAPD[:, :width]))
        if "ckv" in tap:
            @block.gpsimd
            def _(g):
                g.wait_ge(dP.sem, self.ckv_done_dP)
                stO.dma(g.dma_start(out=tap["ckv"][:, :], in_=self.CKV[:, :]))

    def emit_taps_attn(self, wait_vE):
        nc, block = self.nc, self.block
        stO, vE = self.sem("stO"), self.sem("vE")
        if "attn_r" not in self.tap_out:
            return

        @block.sync
        def _(sy):
            sy.wait_ge(vE.sem, wait_vE)
            stO.dma(sy.dma_start(
                out=self.tap_out["attn_r"].rearrange("(tt p) c -> p tt c", p=128),
                in_=self.R[:].rearrange("p (tt c) -> p tt c", tt=8)))

    def emit_taps_ffnprep(self):
        nc, block = self.nc, self.block
        stO, vE, dP, gI = (self.sem(x) for x in ["stO", "vE", "dP", "gI"])
        tap = self.tap_out
        items = [("xnorm8", None, None)]

        if "logits" in tap:
            @block.sync
            def _(sy):
                sy.wait_ge(vE.sem, self.xn_ready_vE)
                stO.dma(sy.dma_start(
                    out=tap["logits"].rearrange("(tt p) e -> p tt e", p=128),
                    in_=self.logits_sb[:].rearrange("p (tt e) -> p tt e", tt=8)))

        if "routew" in tap:
            @block.sync
            def _(sy):
                sy.wait_ge(vE.sem, self.xn_ready_vE)
                stO.dma(sy.dma_start(
                    out=tap["routew"].rearrange("(tt p) k -> p tt k", p=128),
                    in_=self.wids[:].rearrange(
                        "p (tt x) -> p tt x", x=8)[:, :, 4:8]))
                stO.dma(sy.dma_start(
                    out=tap["routeid"].rearrange("(tt p) k -> p tt k", p=128),
                    in_=self.wids[:].rearrange(
                        "p (tt x) -> p tt x", x=8)[:, :, 0:4]))

        if "gbuf" in tap:
            @block.gpsimd
            def _(g):
                g.wait_ge(gI.sem, self.route_done_gI)
                stO.dma(g.dma_start(out=tap["gbuf"][:, :], in_=self.GBUF[:, :]))
                stO.dma(g.dma_start(out=tap["idinfo"][:, :],
                                    in_=self.IDINFO[:, :]))
        if "rsp" in tap:
            @block.gpsimd
            def _(g):
                g.wait_ge(self.sem("dP").sem, self.rspill_done_dP)
                stO.dma(g.dma_start(out=tap["rsp"][:, :], in_=self.RSPILL[:, :]))
        if "shared" in tap:
            @block.gpsimd
            def _(g):
                g.wait_ge(self.sem("dP").sem, self.shared_done_dP)
                stO.dma(g.dma_start(out=tap["shared"][:, :],
                                    in_=self.SHARED[:, :]))

        if "shared" in tap:
            @block.gpsimd
            def _(g):
                g.wait_ge(self.sem("rA").sem, self.shared_done_st[0])
                g.wait_ge(self.sem("rB").sem, self.shared_done_st[1])
                stO.dma(g.dma_start(out=tap["shared"][:, :],
                                    in_=self.SHARED[:, :]))

    def emit_taps_full(self):
        nc, block = self.nc, self.block
        stO, gI = self.sem("stO"), self.sem("gI")
        tap = self.tap_out
        if "contrib" in tap:
            @block.gpsimd
            def _(g):
                g.wait_ge(self.sem("gS").sem, self.moe_done_gI)
                stO.dma(g.dma_start(out=tap["contrib"][:, :],
                                    in_=self.CONTRIB[:, :]))
        if "gbuf" in tap:
            @block.gpsimd
            def _(g):
                g.wait_ge(gI.sem, self.route_done_gI)
                stO.dma(g.dma_start(out=tap["gbuf"][:, :], in_=self.GBUF[:, :]))
                stO.dma(g.dma_start(out=tap["idinfo"][:, :],
                                    in_=self.IDINFO[:, :]))
        if "rsp" in tap:
            @block.gpsimd
            def _(g):
                g.wait_ge(self.sem("dP").sem, self.rspill_done_dP)
                stO.dma(g.dma_start(out=tap["rsp"][:, :], in_=self.RSPILL[:, :]))
        if "shared" in tap:
            @block.gpsimd
            def _(g):
                g.wait_ge(self.sem("dP").sem, self.shared_done_dP)
                stO.dma(g.dma_start(out=tap["shared"][:, :],
                                    in_=self.SHARED[:, :]))

    def finish(self):
        block = self.block
        stO = self.sem("stO")

        @block.sync
        def _(sy):
            sy.wait_ge(stO.sem, stO.val)


def host_layouts(L0_attn_x, L0_attn_Wq_w, L0_attn_Wdkv_w, L0_attn_Wuk_w,
                 L0_attn_Wuv_w, L0_attn_Wo_w, L0_ffn_norm_w, L0_ffn_gate_w,
                 L0_ffn_w13, L0_ffn_w2, L0_ffn_shared_gate_w,
                 L0_ffn_shared_up_w, L0_ffn_shared_down_w):
    xflat = np.ascontiguousarray(np.asarray(L0_attn_x, np.float32).reshape(NT, C))
    Wq = np.asarray(L0_attn_Wq_w, np.float32)
    shared = {
        "xt_tiled": np.ascontiguousarray(
            xflat.reshape(NT // 128, 128, 16, 128).transpose(0, 3, 2, 1)),
        "wq_s": np.ascontiguousarray(
            Wq.reshape(16, 128, 16, 128).transpose(2, 1, 0, 3)),
        "wqt_s": np.ascontiguousarray(
            Wq.reshape(16, 128, 16, 128).transpose(0, 3, 2, 1)),
        "wdkvt_s": np.ascontiguousarray(
            np.asarray(L0_attn_Wdkv_w, np.float32).T.reshape(16, 128, NL)
            .transpose(1, 0, 2)),
        "wuk_s": np.ascontiguousarray(
            np.asarray(L0_attn_Wuk_w, np.float32).reshape(16, 128, NL)
            .transpose(1, 0, 2)),
        "wuv_s": np.ascontiguousarray(
            np.asarray(L0_attn_Wuv_w, np.float32).reshape(16, 128, NL)
            .transpose(1, 0, 2)),
        "wot": np.ascontiguousarray(np.asarray(L0_attn_Wo_w, np.float32).T),
        "gatet_s": np.ascontiguousarray(
            np.asarray(L0_ffn_gate_w, np.float32).T.reshape(16, 128, NE)
            .transpose(1, 0, 2)),
        "normw": np.ascontiguousarray(
            np.asarray(L0_ffn_norm_w, np.float32).reshape(1, C)),
        "sgw_s": np.ascontiguousarray(
            np.asarray(L0_ffn_shared_gate_w, np.float32)
            .reshape(16, 128, 8, 128).transpose(2, 1, 0, 3)),
        "suw_s": np.ascontiguousarray(
            np.asarray(L0_ffn_shared_up_w, np.float32)
            .reshape(16, 128, 8, 128).transpose(2, 1, 0, 3)),
        "sdw": np.ascontiguousarray(np.asarray(L0_ffn_shared_down_w, np.float32)),
        "w13": np.ascontiguousarray(np.asarray(L0_ffn_w13, np.float32)),
        "w2": np.ascontiguousarray(np.asarray(L0_ffn_w2, np.float32)),
    }
    in_maps, own = [], []
    for c in range(NCORE):
        oi = own_index(c)
        own.append(oi)
        xr = np.ascontiguousarray(xflat[oi])
        xtown = np.ascontiguousarray(
            xr.reshape(OWN, 16, 128).transpose(2, 1, 0).reshape(128, 16 * OWN))
        m16 = (np.arange(128)[:, None] <= (16 * c + np.arange(16))[None, :]
               ).astype(np.float32)
        m = dict(shared)
        m["xrows"] = xr
        m["xtown"] = xtown
        m["m16"] = np.ascontiguousarray(m16)
        in_maps.append(m)
    return in_maps, own


def kernel(**inputs):
    in_maps, own = host_layouts(**inputs)
    nc = Build().build()
    res = run_bass_kernel_spmd(nc, in_maps, core_ids=list(range(NCORE)))
    out = np.empty((NT, C), np.float32)
    for c in range(NCORE):
        out[own[c]] = res.results[c]["orows"]
    return out.reshape(B, T, C)
